# version 19
# baseline (speedup 1.0000x reference)
import numpy as np
import ml_dtypes
from contextlib import ExitStack

import concourse.bass as bass
import concourse.mybir as mybir
import concourse.bass_utils as _bu
from concourse.bass_utils import run_bass_kernel_spmd

if not getattr(_bu, "_ldw_opt_patched", False):
    _orig_run_command = _bu.run_command

    def _run_command_ldw(cmd, *a, **kw):
        try:
            cmd = ["--enable-ldw-opt=true" if c == "--enable-ldw-opt=false" else c for c in cmd]
        except TypeError:
            pass
        return _orig_run_command(cmd, *a, **kw)

    _bu.run_command = _run_command_ldw
    _bu._ldw_opt_patched = True

F32 = mybir.dt.float32
BF16 = mybir.dt.bfloat16
I32 = mybir.dt.int32
AF = mybir.ActivationFunctionType
ALU = mybir.AluOpType
AX = mybir.AxisListType

D = 2048
TOK = 1024
NT = 16
HID = 8192
NH, NKV, DH = 16, 4, 128
NSTRIP = 64
NQH = 4
SPQ = 16
EPS_LN = 1e-5
EPS_RMS_FFN = 1e-8
EPS_RMS_ATTN = 1e-6
SCALE = float(1.0 / np.sqrt(DH))
RD = float(1.0 / D)
RDH = float(1.0 / DH)


dbg_p_holder = []


def build_program(n_cores=8, debug=False, serial=0):
    nc = bass.Bass("TRN2", target_bir_lowering=False)
    groups = [[2 * i, 2 * i + 1] for i in range(n_cores // 2)]

    x_d = nc.dram_tensor("x", [D, TOK], F32, kind="ExternalInput").ap()
    out_d = nc.dram_tensor("out", [D, TOK], F32, kind="ExternalOutput").ap()
    wg_d = nc.dram_tensor("wg", [NSTRIP, 128, 2048], F32, kind="ExternalInput").ap()
    wu_d = nc.dram_tensor("wu", [NSTRIP, 128, 2048], F32, kind="ExternalInput").ap()
    wd_d = nc.dram_tensor("wd", [NSTRIP, 128, 2048], F32, kind="ExternalInput").ap()
    wq_d = nc.dram_tensor("wq", [NH, 128, 2048], F32, kind="ExternalInput").ap()
    wk_d = nc.dram_tensor("wk", [NKV, 128, 2048], F32, kind="ExternalInput").ap()
    wv_d = nc.dram_tensor("wv", [NT, 128, 512], F32, kind="ExternalInput").ap()
    wo_d = nc.dram_tensor("wo", [NT, 128, 2048], F32, kind="ExternalInput").ap()
    msk_d = nc.dram_tensor("msk", [8, 128, 512], BF16, kind="ExternalInput").ap()
    mscal_d = nc.dram_tensor("mscal", [1, 8], F32, kind="ExternalInput").ap()
    cs_d = nc.dram_tensor("cs", [2, 128, TOK], F32, kind="ExternalInput").ap()
    swap_d = nc.dram_tensor("swap", [128, 128], F32, kind="ExternalInput").ap()
    sg_g = nc.dram_tensor("sg_g", [NSTRIP, 128, 2048], BF16).ap()
    sg_u = nc.dram_tensor("sg_u", [NSTRIP, 128, 2048], BF16).ap()
    sg_d2 = nc.dram_tensor("sg_d2", [NSTRIP, 128, 2048], BF16).ap()
    CCN = NKV * 128 * TOK + 8 * 128 * 512
    cc_in = nc.dram_tensor("cc_in", [CCN], BF16).ap()
    cc_out = nc.dram_tensor("cc_out", [2 * CCN], BF16).ap()
    dbg = {}
    if debug:
        for nm, shp, dt in [("d_x1", [D, TOK], F32), ("d_ln2", [D, TOK], BF16),
                            ("d_q", [D, TOK], BF16), ("d_kb", [NKV * 128, 2048], BF16),
                            ("d_vb", [16 * 128, 512], BF16), ("d_y", [D, TOK], BF16),
                            ("d_x2", [D, TOK], F32), ("d_xq", [D, TOK], BF16),
                            ("d_rows", [1, 8, 512], F32), ("d_fold", [128, TOK], F32),
                            ("d_sgg", [NSTRIP, 128, 2048], BF16), ("d_sgd", [NSTRIP, 128, 2048], BF16),
                            ("d_p", [128, SPQ, TOK], BF16)]:
            dbg[nm] = nc.dram_tensor(nm, shp, dt, kind="ExternalOutput").ap()

    cnt = {}

    def inc(s, n=1):
        cnt[s] = cnt.get(s, 0) + n
        return cnt[s]

    def cur(s):
        return cnt.get(s, 0)

    SEMS = (["sx", "sw", "swc", "sscr", "smm", "sve", "sse", "sgp", "scc", "sbo", "sms", "sst", "smx0", "smx1"]
            + ["sxt%d" % i for i in range(4)]
            + ["swr0", "swr1", "sqw0", "sqw1"]
            + ["scr_%s_%d" % (b, p) for b in ("wbf_a", "wbf_b") for p in (0, 1)]
            + ["swb_%s_%d" % (b, p) for b in ("wbf_a", "wbf_b") for p in (0, 1, 2)]
            + ["scr_%s_2" % b for b in ("wbf_a", "wbf_b")])

    with ExitStack() as ES:
        sem = {k: ES.enter_context(nc.semaphore(k)) for k in SEMS}
        x_sb = ES.enter_context(nc.sbuf_tensor("x_sb", [128, NT, TOK], F32))
        rows = ES.enter_context(nc.sbuf_tensor("rows", [1, 8, 512], F32))
        ones_c = ES.enter_context(nc.sbuf_tensor("ones_c", [128, 1], F32))
        ones_r = ES.enter_context(nc.sbuf_tensor("ones_r", [1, 128], F32))
        ones_cb = ES.enter_context(nc.sbuf_tensor("ones_cb", [128, 1], BF16))
        wraw = ES.enter_context(nc.sbuf_tensor("wraw", [128, 2, 1024], F32))
        tmpP = ES.enter_context(nc.sbuf_tensor("tmpP", [128, 512], F32))
        wbf_a = ES.enter_context(nc.sbuf_tensor("wbf_a", [128, 3, 2048], BF16))
        wbf_b = ES.enter_context(nc.sbuf_tensor("wbf_b", [128, 3, 2048], BF16))
        tmpA = ES.enter_context(nc.sbuf_tensor("tmpA", [128, 512], F32))
        tmpB = ES.enter_context(nc.sbuf_tensor("tmpB", [128, 512], F32))
        slu = ES.enter_context(nc.sbuf_tensor("slu", [128, 2, 512], BF16))
        cst = ES.enter_context(nc.sbuf_tensor("cst", [128, 4], F32))
        ident = ES.enter_context(nc.sbuf_tensor("ident", [128, 128], F32))
        qint = ES.enter_context(nc.sbuf_tensor("qint", [128, 512], I32))
        fold = ES.enter_context(nc.sbuf_tensor("fold", [128, TOK], F32))
        block = ES.enter_context(nc.Block())
        pw = [ES.enter_context(nc.psum_tensor("pw%d" % i, [128, 512], F32)) for i in range(6)]
        pr = [ES.enter_context(nc.psum_tensor("pr%d" % i, [128, 512], F32)) for i in range(2)]

        R_S1, R_S2, R_T0, R_T1, R_A, R_B, R_M, R_M0 = range(8)
        R_AS, R_BS = R_S1, R_S2
        R_R0, R_R1 = R_S1, R_S2
        R_M1 = R_M

        def row(i):
            return rows[0:1, i, :]

        wchunk = [0]
        buf_free = {}
        scr_free = {}

        def stream_strip(src, sidx, dst, parity, mode, scratch=None, free=2048):
            key = (dst.name, parity)
            slot_sem = "swb_%s_%d" % (dst.name, parity)
            if mode == "bf16":
                gate = buf_free.get(key, 0)

                @block.sync
                def _(sync, gate=gate, slot_sem=slot_sem):
                    if gate:
                        sync.wait_ge(sem["smm"], gate)
                    sync.dma_start(out=dst[:, parity, 0:free], in_=src[sidx]).then_inc(sem[slot_sem], 16)
                mark = inc(slot_sem, 16)
                return (slot_sem, mark)
            nch = max(1, free // 1024)
            cw = min(free, 1024)
            c0 = wchunk[0]
            dma_marks = []
            for ch in range(nch):
                sn = "swr%d" % ((c0 + ch) % 2)
                dma_marks.append((sn, inc(sn, 16)))

            @block.sync
            def _(sync, c0=c0, dma_marks=tuple(dma_marks), nch=nch, cw=cw):
                for ch in range(nch):
                    n = c0 + ch
                    if n >= 2:
                        sync.wait_ge(sem["swc"], n - 1)
                    sync.dma_start(out=wraw[:, n % 2, 0:cw],
                                   in_=src[sidx, :, ch * cw:(ch + 1) * cw]).then_inc(
                        sem[dma_marks[ch][0]], 16)
            gate_mm = buf_free.get(key, 0)
            gate_scr = scr_free.get(key, 0)

            @block.vector
            def _(vector, c0=c0, dma_marks=tuple(dma_marks), gate_mm=gate_mm, gate_scr=gate_scr,
                  nch=nch, cw=cw, ser_mm=(cur("smm") if serial else 0)):
                for ch in range(nch):
                    vector.wait_ge(sem[dma_marks[ch][0]], dma_marks[ch][1])
                    if ch == 0:
                        if gate_mm:
                            vector.wait_ge(sem["smm"], gate_mm)
                        if gate_scr:
                            vector.wait_ge(sem[gate_scr[0]], gate_scr[1])
                        if ser_mm:
                            vector.wait_ge(sem["smm"], ser_mm)
                    dstap = dst[:, parity, ch * cw:(ch + 1) * cw]
                    srcap = wraw[:, (c0 + ch) % 2, 0:cw]
                    if mode == "sign":
                        vector.tensor_scalar(srcap, srcap, 1e38, 1.0, ALU.mult, ALU.min)
                        vector.tensor_scalar_max(dstap, srcap, -1.0).then_inc(sem["swc"], 1)
                    else:
                        vector.tensor_copy(dstap, srcap).then_inc(sem["swc"], 1)
            for ch in range(nch):
                mark = inc("swc", 1)
            wchunk[0] = c0 + nch
            if scratch is not None:
                scr_sem = "scr_%s_%d" % (dst.name, parity)

                @block.gpsimd
                def _(gpsimd, mark=mark, scr_sem=scr_sem):
                    gpsimd.wait_ge(sem["swc"], mark)
                    gpsimd.dma_start(out=scratch[sidx], in_=dst[:, parity, :]).then_inc(sem[scr_sem], 16)
                scr_free[key] = (scr_sem, inc(scr_sem, 16))
            return ("swc", mark)

        @block.vector
        def _(vector):
            vector.memset(ones_c[:], 1.0)
            vector.memset(ones_cb[:], 1.0)
            vector.memset(cst[:, 0:1], EPS_RMS_ATTN)
            vector.memset(cst[:, 1:2], EPS_RMS_ATTN * DH)
            vector.memset(cst[:, 2:3], RDH)
            vector.memset(cst[:, 3:4], 1.0)
            vector.memset(ones_r[:], 1.0).then_inc(sem["sve"], 1)
        inc("sve", 1)
        ones_ready = cur("sve")

        @block.gpsimd
        def _(gpsimd):
            for kt in range(NT):
                gpsimd.dma_start(out=x_sb[:, kt, :],
                                 in_=x_d[kt * 128:(kt + 1) * 128, :]).then_inc(
                    sem["sxt%d" % (kt % 4)], 16)
        for kt in range(NT):
            inc("sxt%d" % (kt % 4), 16)

        @block.sync
        def _(sync):
            sync.dma_start(out=ident[:], in_=swap_d).then_inc(sem["sx"], 16)
        inc("sx", 16)
        x_ready = cur("sx")

        def x_tile_wait(eng, kt):
            eng.wait_ge(sem["sxt%d" % (kt % 4)], 16 * (kt // 4 + 1))

        mx_count = {"smx0": 0, "smx1": 0}


        def emit_stats(extra_waits):
            raise RuntimeError("inlined")

        def emit_lnq(xq_sb, extra_waits):
            xq_marks = []
            ab_bank = {0: (4, 5), 1: (2, 3)}
            vfold = {}
            vcp = {}
            mx_mark = {}
            stats_mm = {}
            w6m = {}

            def stats_and_algebra(tc, gate_extra, gate_tensor=()):
                ts = slice(tc * 512, (tc + 1) * 512)
                se0 = cur("sse")
                mm0 = cur("smm")
                ve_gate = cur("sve")

                sqb = (tmpA[:], tmpB[:], fold[:, ts])

                @block.scalar
                def _(scalar, ts=ts, mm0=mm0, ew=tuple(extra_waits), ge=tuple(gate_extra)):
                    for sname, sval in ew:
                        scalar.wait_ge(sem[sname], sval)
                    scalar.wait_ge(sem["sx"], x_ready)
                    for sname, sval in ge:
                        scalar.wait_ge(sem[sname], sval)
                    for kt in range(NT):
                        x_tile_wait(scalar, kt)
                        if kt >= 3:
                            scalar.wait_ge(sem["smm"], mm0 + kt - 2)
                        scalar.activation(sqb[kt % 3], x_sb[:, kt, ts], AF.Square).then_inc(sem["sse"], 1)
                inc("sse", NT)

                @block.tensor
                def _(tensor, ts=ts, se0=se0, ew=tuple(extra_waits), ge=tuple(gate_extra),
                      gt=tuple(gate_tensor)):
                    for sname, sval in ew:
                        tensor.wait_ge(sem[sname], sval)
                    tensor.wait_ge(sem["sx"], x_ready)
                    tensor.wait_ge(sem["sve"], ones_ready)
                    for sname, sval in ge:
                        tensor.wait_ge(sem[sname], sval)
                    for sname, sval in gt:
                        tensor.wait_ge(sem[sname], sval)
                    for kt in range(NT):
                        x_tile_wait(tensor, kt)
                        tensor.matmul(pr[0][0:1, :], ones_c[:], x_sb[:, kt, ts],
                                      start=(kt == 0), stop=(kt == NT - 1))
                        tensor.wait_ge(sem["sse"], se0 + kt + 1)
                        tensor.matmul(pr[1][0:1, :], ones_c[:], sqb[kt % 3],
                                      start=(kt == 0), stop=(kt == NT - 1)).then_inc(sem["smm"], 1)
                inc("smm", NT)
                mm_stats = cur("smm")
                stats_mm[tc] = mm_stats

                @block.vector
                def _(vector, mm_stats=mm_stats):
                    vector.wait_ge(sem["smm"], mm_stats)
                    vector.tensor_copy(row(R_S1), pr[0][0:1, :])
                    vector.tensor_copy(row(R_S2), pr[1][0:1, :])
                    vector.tensor_scalar_mul(row(R_T0), row(R_S1), RD)
                    vector.tensor_scalar_mul(row(R_T1), row(R_S2), RD)
                    vector.tensor_mul(row(R_A), row(R_T0), row(R_T0))
                    vector.tensor_sub(row(R_M), row(R_T1), row(R_A))
                    vector.tensor_scalar_add(row(R_T1), row(R_M), EPS_LN).then_inc(sem["sve"], 1)
                w1 = inc("sve", 1)

                @block.scalar
                def _(scalar, w1=w1):
                    scalar.wait_ge(sem["sve"], w1)
                    scalar.activation(row(R_T1), row(R_T1), AF.Ln)
                    scalar.activation(row(R_A), row(R_T1), AF.Exp, scale=-0.5).then_inc(sem["sse"], 1)
                w2 = inc("sse", 1)

                @block.vector
                def _(vector, w2=w2):
                    vector.wait_ge(sem["sse"], w2)
                    vector.tensor_mul(row(R_T1), row(R_A), row(R_A))
                    vector.tensor_mul(row(R_T1), row(R_T1), row(R_M))
                    vector.tensor_scalar_add(row(R_B), row(R_T1), EPS_RMS_FFN).then_inc(sem["sve"], 1)
                w3 = inc("sve", 1)

                @block.scalar
                def _(scalar, w3=w3):
                    scalar.wait_ge(sem["sve"], w3)
                    scalar.activation(row(R_B), row(R_B), AF.Ln)
                    scalar.activation(row(R_B), row(R_B), AF.Exp, scale=-0.5).then_inc(sem["sse"], 1)
                w4 = inc("sse", 1)

                @block.vector
                def _(vector, w4=w4):
                    vector.wait_ge(sem["sse"], w4)
                    vector.tensor_mul(row(R_AS), row(R_A), row(R_B))
                    vector.scalar_tensor_tensor(row(R_BS), row(R_T0), -1.0, row(R_AS),
                                                ALU.mult, ALU.mult).then_inc(sem["sve"], 1)
                w5 = inc("sve", 1)

                ba, bb = ab_bank[tc]

                @block.tensor
                def _(tensor, w5=w5, ba=ba, bb=bb):
                    tensor.wait_ge(sem["sve"], w5)
                    tensor.matmul(pw[ba][:], ones_r[:], row(R_AS), start=True, stop=True)
                    tensor.matmul(pw[bb][:], ones_r[:], row(R_BS), start=True, stop=True).then_inc(sem["smm"], 1)
                w6m[tc] = inc("smm", 1)

            def pass_a(tc):
                ts = slice(tc * 512, (tc + 1) * 512)
                ba, bb = ab_bank[tc]

                @block.vector
                def _(vector, ts=ts, w6=w6m[tc], ba=ba, bb=bb):
                    vector.wait_ge(sem["smm"], w6)
                    for kt in range(NT):
                        vector.tensor_mul(tmpP[:], x_sb[:, kt, ts], pw[ba][:])
                        vector.tensor_add(tmpP[:], tmpP[:], pw[bb][:])
                        if kt == 0:
                            vector.scalar_tensor_tensor(fold[:, ts], tmpP[:], -1.0, tmpP[:],
                                                        ALU.mult, ALU.max)
                        else:
                            vector.tensor_max(fold[:, ts], fold[:, ts], tmpP[:])
                            if kt == NT - 1:
                                vector.scalar_tensor_tensor(fold[:, ts], tmpP[:], -1.0, fold[:, ts],
                                                            ALU.mult, ALU.max).then_inc(sem["sve"], 1)
                            else:
                                vector.scalar_tensor_tensor(fold[:, ts], tmpP[:], -1.0, fold[:, ts],
                                                            ALU.mult, ALU.max)
                vfold[tc] = inc("sve", 1)

            def m_chain(tc):
                mrow = R_M0 if tc == 0 else R_M1
                mxsem = "smx0" if tc == 0 else "smx1"

                @block.tensor
                def _(tensor, tc=tc, vf=vfold[tc]):
                    tensor.wait_ge(sem["sve"], vf)
                    for j in range(4):
                        tensor.matmul(pr[tc][:, j * 128:(j + 1) * 128],
                                      fold[:, tc * 512 + j * 128:tc * 512 + (j + 1) * 128],
                                      ident[:], start=True, stop=True).then_inc(sem["smm"], 1)
                ttr = inc("smm", 4)

                f0 = tc * 512
                mc = f0 + 128

                @block.vector
                def _(vector, tc=tc, ttr=ttr, mc=mc):
                    vector.wait_ge(sem["smm"], ttr)
                    for j in range(4):
                        mx = vector.tensor_reduce(fold[:, mc + j:mc + j + 1],
                                                  pr[tc][:, j * 128:(j + 1) * 128], AX.X, ALU.max)
                        if j == 3:
                            mx.then_inc(sem["sve"], 1)
                vmax = inc("sve", 1)

                @block.tensor
                def _(tensor, tc=tc, vmax=vmax, mc=mc):
                    tensor.wait_ge(sem["sve"], vmax)
                    tensor.matmul(pr[tc][0:4, 0:128], fold[:, mc:mc + 4], ident[:],
                                  start=True, stop=True).then_inc(sem["smm"], 1)
                tback = inc("smm", 1)

                @block.vector
                def _(vector, tc=tc, tback=tback, f0=f0):
                    vector.wait_ge(sem["smm"], tback)
                    vector.tensor_copy(fold[0:4, f0:f0 + 128], pr[tc][0:4, 0:128]).then_inc(sem["sve"], 1)
                vcp[tc] = inc("sve", 1)

                @block.gpsimd
                def _(gpsimd, vc=vcp[tc], mrow=mrow, mxsem=mxsem, f0=f0):
                    gpsimd.wait_ge(sem["sve"], vc)
                    for j in range(4):
                        gpsimd.dma_start(out=rows[0:1, mrow, j * 128:(j + 1) * 128],
                                         in_=fold[j:j + 1, f0:f0 + 128]).then_inc(sem[mxsem], 16)
                mx_count[mxsem] += 64
                mx_mark[tc] = (mxsem, mx_count[mxsem])

            def pass_b(tc, prev_b_mark):
                ts = slice(tc * 512, (tc + 1) * 512)
                ba, bb = ab_bank[tc]
                mrow = R_M0 if tc == 0 else R_M1

                s_row = tmpA[0:1, 0:512]
                i_row = tmpB[0:1, 0:512]

                sq_gate = stats_mm.get(1, 0) if tc == 0 else 0

                @block.scalar
                def _(scalar, g=mx_mark[tc], pb=prev_b_mark, mrow=mrow, s_row=s_row,
                      sq_gate=sq_gate):
                    if sq_gate:
                        scalar.wait_ge(sem["smm"], sq_gate)
                    scalar.wait_ge(sem[g[0]], g[1])
                    if pb:
                        scalar.wait_ge(sem["sve"], pb)
                    scalar.activation(s_row, row(mrow), AF.Ln)
                    scalar.activation(s_row, s_row, AF.Exp, scale=-1.0).then_inc(sem["sse"], 1)
                sm = inc("sse", 1)

                @block.vector
                def _(vector, sm=sm, mrow=mrow, s_row=s_row, i_row=i_row, ba=ba, bb=bb):
                    vector.wait_ge(sem["sse"], sm)
                    vector.scalar_tensor_tensor(row(R_T0), pw[ba][0:1, :], 127.0, s_row,
                                                ALU.mult, ALU.mult)
                    vector.scalar_tensor_tensor(row(R_T1), pw[bb][0:1, :], 127.0, s_row,
                                                ALU.mult, ALU.mult)
                    vector.tensor_scalar_mul(i_row, row(mrow), 1.0 / 127.0).then_inc(sem["sve"], 1)
                srow = inc("sve", 1)

                @block.tensor
                def _(tensor, srow=srow, i_row=i_row, ba=ba, bb=bb):
                    tensor.wait_ge(sem["sve"], srow)
                    tensor.matmul(pw[ba][:], ones_r[:], row(R_T0), start=True, stop=True)
                    tensor.matmul(pw[bb][:], ones_r[:], row(R_T1), start=True, stop=True)
                    tensor.matmul(pw[1][:], ones_r[:], i_row, start=True, stop=True).then_inc(sem["smm"], 1)
                sbc = inc("smm", 1)

                @block.vector
                def _(vector, ts=ts, sbc=sbc, ba=ba, bb=bb, xq_sb=xq_sb):
                    vector.wait_ge(sem["smm"], sbc)
                    for kt in range(NT):
                        vector.tensor_mul(tmpP[:], x_sb[:, kt, ts], pw[ba][:])
                        vector.tensor_add(tmpP[:], tmpP[:], pw[bb][:])
                        vector.tensor_copy(qint[:], tmpP[:])
                        if kt == NT - 1:
                            vector.tensor_mul(xq_sb[:, kt, ts], qint[:], pw[1][:]).then_inc(sem["sve"], 1)
                        else:
                            vector.tensor_mul(xq_sb[:, kt, ts], qint[:], pw[1][:])
                xq_marks.append(inc("sve", 1))

            stats_and_algebra(0, [])
            pa_gate = cur("sve")
            pass_a(0)
            stats_and_algebra(1, [("sve", pa_gate)])
            m_chain(0)
            pass_a(1)
            pass_b(0, 0)
            m_chain(1)
            pass_b(1, xq_marks[0])
            return tuple(xq_marks)

        def emit_ffn_mats(xq_sb, p_sb, first_pass, xq_ready_mark):
            mm3_last_q = 0
            for qh in range(NQH):
                pend = []
                pevac = {}
                last_pevac = {0: 0, 1: 0}

                def compute_strip(si, wg_rdy, wu_rdy, par):
                    mmgu = []

                    @block.tensor
                    def _(tensor, si=si, wg_rdy=wg_rdy, wu_rdy=wu_rdy, par=par,
                          lp0=last_pevac[0], lp1=last_pevac[1],
                          serw=(cur("swc") if serial else 0), serv=(cur("sve") if serial else 0)):
                        tensor.wait_ge(sem[wg_rdy[0]], wg_rdy[1])
                        tensor.wait_ge(sem[wu_rdy[0]], wu_rdy[1])
                        if serw:
                            tensor.wait_ge(sem["swc"], serw)
                        if serv:
                            tensor.wait_ge(sem["sve"], serv)
                        tensor.wait_ge(sem["sve"], xq_ready_mark[1])
                        tensor.wait_ge(sem["sve"], lp0)
                        tensor.wait_ge(sem["sve"], lp1)
                        for kt in range(NT):
                            for tc in range(2):
                                mm = tensor.matmul(pw[2 * tc][:], wbf_a[:, par, kt * 128:(kt + 1) * 128],
                                                   xq_sb[:, kt, tc * 512:(tc + 1) * 512],
                                                   start=(kt == 0), stop=(kt == NT - 1))
                            if kt == NT - 1:
                                mm.then_inc(sem["smm"], 1)
                        for kt in range(NT):
                            for tc in range(2):
                                mm = tensor.matmul(pw[2 * tc + 1][:], wbf_b[:, par, kt * 128:(kt + 1) * 128],
                                                   xq_sb[:, kt, tc * 512:(tc + 1) * 512],
                                                   start=(kt == 0), stop=(kt == NT - 1))
                            if kt == NT - 1:
                                mm.then_inc(sem["smm"], 1)
                    mmgu.append(inc("smm", 1))
                    mmgu.append(inc("smm", 1))
                    buf_free[("wbf_a", par)] = mmgu[1]
                    buf_free[("wbf_b", par)] = mmgu[1]

                    @block.scalar
                    def _(scalar, mmgu=tuple(mmgu)):
                        scalar.wait_ge(sem["smm"], mmgu[0])
                        scalar.activation(slu[:, 0, :], pw[0][:], AF.Silu).then_inc(sem["sse"], 1)
                        scalar.wait_ge(sem["smm"], mmgu[1])
                        scalar.activation(slu[:, 1, :], pw[2][:], AF.Silu).then_inc(sem["sse"], 1)
                    silu0 = inc("sse", 1)
                    silu1 = inc("sse", 1)

                    @block.vector
                    def _(vector, si=si, silu0=silu0, silu1=silu1, mm3q=mm3_last_q, p_sb=p_sb, mmgu2=tuple(mmgu)):
                        if mm3q:
                            vector.wait_ge(sem["smm"], mm3q)
                        vector.wait_ge(sem["sse"], silu0)
                        vector.wait_ge(sem["smm"], mmgu2[1])
                        vector.tensor_mul(p_sb[:, si, 0:512], slu[:, 0, :], pw[1][:]).then_inc(sem["sve"], 1)
                        vector.wait_ge(sem["sse"], silu1)
                        vector.tensor_mul(p_sb[:, si, 512:1024], slu[:, 1, :], pw[3][:]).then_inc(sem["sve"], 1)
                    last_pevac[0] = inc("sve", 1)
                    last_pevac[1] = inc("sve", 1)

                for si in range(SPQ):
                    s = qh * SPQ + si
                    par = si % 3
                    if first_pass:
                        wg_rdy = stream_strip(wg_d, s, wbf_a, par, "sign", scratch=sg_g)
                        wu_rdy = stream_strip(wu_d, s, wbf_b, par, "sign", scratch=sg_u)
                    else:
                        wg_rdy = stream_strip(sg_g, s, wbf_a, par, "bf16")
                        wu_rdy = stream_strip(sg_u, s, wbf_b, par, "bf16")
                    pend.append((si, wg_rdy, wu_rdy, par))
                    if len(pend) == 3:
                        compute_strip(*pend.pop(0))
                while pend:
                    compute_strip(*pend.pop(0))
                p_ready = cur("sve")

                pendd = []
                last_ev = 0
                for dt in range(NT):
                    s = qh * SPQ + dt
                    par = dt % 3
                    if first_pass:
                        wd_rdy = stream_strip(wd_d, s, wbf_a, par, "sign", scratch=sg_d2)
                    else:
                        wd_rdy = stream_strip(sg_d2, s, wbf_a, par, "bf16")
                    pendd.append((dt, wd_rdy, par))

                    def compute_dt(dt, wd_rdy, par):
                        nonlocal last_ev
                        mm3 = []

                        @block.tensor
                        def _(tensor, dt=dt, wd_rdy=wd_rdy, par=par, p_ready=p_ready, last_ev=last_ev, p_sb=p_sb,
                              serw=(cur("swc") if serial else 0), serv=(cur("sve") if serial else 0)):
                            tensor.wait_ge(sem[wd_rdy[0]], wd_rdy[1])
                            if serw:
                                tensor.wait_ge(sem["swc"], serw)
                            if serv:
                                tensor.wait_ge(sem["sve"], serv)
                            tensor.wait_ge(sem["sve"], p_ready)
                            if last_ev:
                                tensor.wait_ge(sem["sve"], last_ev)
                            for si2 in range(SPQ):
                                for tc in range(2):
                                    mm = tensor.matmul(pw[4 + tc][:], wbf_a[:, par, si2 * 128:(si2 + 1) * 128],
                                                       p_sb[:, si2, tc * 512:(tc + 1) * 512],
                                                       start=(si2 == 0), stop=(si2 == SPQ - 1))
                                    if si2 == SPQ - 1:
                                        mm.then_inc(sem["smm"], 1)
                        mm3.append(inc("smm", 1))
                        mm3.append(inc("smm", 1))
                        buf_free[("wbf_a", par)] = mm3[1]

                        @block.vector
                        def _(vector, dt=dt, mm3=tuple(mm3)):
                            vector.wait_ge(sem["smm"], mm3[0])
                            vector.scalar_tensor_tensor(x_sb[:, dt, 0:512], pw[4][:], 0.5,
                                                        x_sb[:, dt, 0:512], ALU.mult, ALU.add)
                            vector.wait_ge(sem["smm"], mm3[1])
                            vector.scalar_tensor_tensor(x_sb[:, dt, 512:1024], pw[5][:], 0.5,
                                                        x_sb[:, dt, 512:1024], ALU.mult, ALU.add).then_inc(sem["sve"], 1)
                        last_ev = inc("sve", 1)

                        if not first_pass and qh == NQH - 1:
                            @block.gpsimd
                            def _(gpsimd, dt=dt, ev=last_ev):
                                gpsimd.wait_ge(sem["sve"], ev)
                                gpsimd.dma_start(out=out_d[dt * 128:(dt + 1) * 128, :],
                                                 in_=x_sb[:, dt, :]).then_inc(sem["sst"], 16)
                            inc("sst", 16)

                    if len(pendd) == 3:
                        compute_dt(*pendd.pop(0))
                while pendd:
                    compute_dt(*pendd.pop(0))
                mm3_last_q = cur("smm")
            return cur("sve")

        with nc.sbuf_tensor("xq1", [128, NT, TOK], BF16) as xq1, \
             nc.sbuf_tensor("p1", [128, SPQ, TOK], BF16) as p1:
            xq1_ready = emit_lnq(xq1, [])
            dbg_p_holder.clear()
            dbg_p_holder.append(p1)
            if debug:
                @block.gpsimd
                def _(gpsimd, m=xq1_ready[1]):
                    gpsimd.wait_ge(sem["sve"], m)
                    gpsimd.dma_start(out=dbg["d_xq"].rearrange("(t p) m -> p t m", p=128),
                                     in_=xq1[:, :, :]).then_inc(sem["sbo"], 16)
                    gpsimd.dma_start(out=dbg["d_rows"], in_=rows[:, :, :]).then_inc(sem["sbo"], 16)
                    gpsimd.dma_start(out=dbg["d_fold"], in_=fold[:, :]).then_inc(sem["sbo"], 16)
                inc("sbo", 48)
            ffn1_done = emit_ffn_mats(xq1, p1, True, xq1_ready)
            if debug:
                @block.gpsimd
                def _(gpsimd, m=ffn1_done):
                    gpsimd.wait_ge(sem["sve"], m)
                    gpsimd.dma_start(out=dbg["d_p"], in_=p1[:, :, :]).then_inc(sem["sbo"], 16)
                inc("sbo", 16)

        if debug:
            @block.gpsimd
            def _(gpsimd, m=ffn1_done, scr=cur("sscr")):
                gpsimd.wait_ge(sem["sve"], m)
                gpsimd.wait_ge(sem["sscr"], scr)
                gpsimd.dma_start(out=dbg["d_x1"].rearrange("(t p) m -> p t m", p=128),
                                 in_=x_sb[:, :, :]).then_inc(sem["sbo"], 16)
                gpsimd.dma_start(out=dbg["d_sgg"], in_=sg_g[:, :, :]).then_inc(sem["sbo"], 16)
                gpsimd.dma_start(out=dbg["d_sgd"], in_=sg_d2[:, :, :]).then_inc(sem["sbo"], 16)
            inc("sbo", 48)

        with nc.sbuf_tensor("qy", [128, NH, TOK], BF16) as qy:
            rope_marks = [0]
            proj_ev = [0]
            with nc.sbuf_tensor("ln2", [128, NT, TOK], BF16) as ln2:
                ln2_ab = {0: (4, 5), 1: (2, 3)}
                for tc in range(2):
                    ts = slice(tc * 512, (tc + 1) * 512)
                    se0 = cur("sse")
                    mm0 = cur("smm")
                    ve_gate = cur("sve")

                    sqb = (tmpA[:], tmpB[:], fold[:, ts])

                    @block.scalar
                    def _(scalar, ts=ts, mm0=mm0, fd=ffn1_done, ve_gate=ve_gate):
                        scalar.wait_ge(sem["sve"], fd)
                        scalar.wait_ge(sem["sve"], ve_gate)
                        for kt in range(NT):
                            if kt >= 3:
                                scalar.wait_ge(sem["smm"], mm0 + kt - 2)
                            scalar.activation(sqb[kt % 3], x_sb[:, kt, ts], AF.Square).then_inc(sem["sse"], 1)
                    inc("sse", NT)

                    @block.tensor
                    def _(tensor, ts=ts, se0=se0, fd=ffn1_done, ve_gate=ve_gate):
                        tensor.wait_ge(sem["sve"], fd)
                        tensor.wait_ge(sem["sve"], ve_gate)
                        for kt in range(NT):
                            tensor.matmul(pr[0][0:1, :], ones_c[:], x_sb[:, kt, ts],
                                          start=(kt == 0), stop=(kt == NT - 1))
                            tensor.wait_ge(sem["sse"], se0 + kt + 1)
                            tensor.matmul(pr[1][0:1, :], ones_c[:], sqb[kt % 3],
                                          start=(kt == 0), stop=(kt == NT - 1)).then_inc(sem["smm"], 1)
                    inc("smm", NT)
                    mm_stats = cur("smm")

                    @block.vector
                    def _(vector, mm_stats=mm_stats):
                        vector.wait_ge(sem["smm"], mm_stats)
                        vector.tensor_copy(row(R_S1), pr[0][0:1, :])
                        vector.tensor_copy(row(R_S2), pr[1][0:1, :])
                        vector.tensor_scalar_mul(row(R_T0), row(R_S1), RD)
                        vector.tensor_scalar_mul(row(R_T1), row(R_S2), RD)
                        vector.tensor_mul(row(R_A), row(R_T0), row(R_T0))
                        vector.tensor_sub(row(R_T1), row(R_T1), row(R_A))
                        vector.tensor_scalar_add(row(R_T1), row(R_T1), EPS_LN).then_inc(sem["sve"], 1)
                    w1 = inc("sve", 1)

                    @block.scalar
                    def _(scalar, w1=w1):
                        scalar.wait_ge(sem["sve"], w1)
                        scalar.activation(row(R_T1), row(R_T1), AF.Ln)
                        scalar.activation(row(R_A), row(R_T1), AF.Exp, scale=-0.5).then_inc(sem["sse"], 1)
                    w2 = inc("sse", 1)

                    @block.vector
                    def _(vector, w2=w2):
                        vector.wait_ge(sem["sse"], w2)
                        vector.scalar_tensor_tensor(row(R_B), row(R_T0), -1.0, row(R_A),
                                                    ALU.mult, ALU.mult).then_inc(sem["sve"], 1)
                    w3 = inc("sve", 1)

                    ba2, bb2 = ln2_ab[tc]

                    @block.tensor
                    def _(tensor, w3=w3, ba2=ba2, bb2=bb2):
                        tensor.wait_ge(sem["sve"], w3)
                        tensor.matmul(pw[ba2][:], ones_r[:], row(R_A), start=True, stop=True)
                        tensor.matmul(pw[bb2][:], ones_r[:], row(R_B), start=True, stop=True).then_inc(sem["smm"], 1)
                    w4 = inc("smm", 1)

                    @block.vector
                    def _(vector, ts=ts, w4=w4, ba2=ba2, bb2=bb2):
                        vector.wait_ge(sem["smm"], w4)
                        for kt in range(NT):
                            vector.tensor_mul(tmpP[:], x_sb[:, kt, ts], pw[ba2][:])
                            if kt == NT - 1:
                                vector.tensor_add(ln2[:, kt, ts], tmpP[:], pw[bb2][:]).then_inc(sem["sve"], 1)
                            else:
                                vector.tensor_add(ln2[:, kt, ts], tmpP[:], pw[bb2][:])
                    inc("sve", 1)
                ln2_done = cur("sve")

                if debug:
                    @block.gpsimd
                    def _(gpsimd, m=ln2_done):
                        gpsimd.wait_ge(sem["sve"], m)
                        gpsimd.dma_start(out=dbg["d_ln2"].rearrange("(t p) m -> p t m", p=128),
                                         in_=ln2[:, :, :]).then_inc(sem["sbo"], 16)
                    inc("sbo", 16)

                with nc.sbuf_tensor("cs_sb", [128, 2, TOK], BF16) as cs_sb, \
                     nc.sbuf_tensor("ktmp", [128, TOK], BF16) as ktmp, \
                     nc.sbuf_tensor("qsw", [128, 2, TOK], BF16) as qsw, \
                     nc.sbuf_tensor("tb16a", [128, 512], BF16) as tb16a, \
                     nc.sbuf_tensor("tb16b", [128, 512], BF16) as tb16b, \
                     nc.sbuf_tensor("vtmp", [128, 512], BF16) as vtmp, \
                     nc.sbuf_tensor("wvb", [128, 2, 512], BF16) as wvb:

                    @block.gpsimd
                    def _(gpsimd):
                        gpsimd.dma_start(out=cs_sb[:, :, :],
                                         in_=cs_d.rearrange("a p m -> p a m")).then_inc(sem["sms"], 16)
                    inc("sms", 16)
                    cs_ready = cur("sms")

                    def head_conv(h, is_k, ci):
                        par = ci % 3
                        return par, stream_strip(wk_d if is_k else wq_d, h, wbf_a, par, "copy")

                    def head_mm(hi, h, is_k, par, w_rdy, extra=()):
                        pp = hi % 2

                        @block.tensor
                        def _(tensor, h=h, par=par, pp=pp, w_rdy=w_rdy, extra=tuple(extra)):
                            tensor.wait_ge(sem[w_rdy[0]], w_rdy[1])
                            tensor.wait_ge(sem["sve"], ln2_done)
                            for sn, sv in extra:
                                tensor.wait_ge(sem[sn], sv)
                            for kt in range(NT):
                                for tc in range(2):
                                    mm = tensor.matmul(pw[2 * pp + tc][:],
                                                       wbf_a[:, par, kt * 128:(kt + 1) * 128],
                                                       ln2[:, kt, tc * 512:(tc + 1) * 512],
                                                       start=(kt == 0), stop=(kt == NT - 1))
                                    if kt == NT - 1 and tc == 1:
                                        mm.then_inc(sem["smm"], 1)
                        mmq = inc("smm", 1)
                        buf_free[("wbf_a", par)] = mmq
                        return mmq

                    def head_chain(hi, h, is_k, mmq, kship_gate=0):
                        pp = hi % 2
                        sl = hi % 2
                        sqwsem = "sqw%d" % sl

                        def t(ts2):
                            return ktmp[:, ts2] if is_k else qy[:, h, ts2]

                        @block.vector
                        def _(vector, h=h, pp=pp, mmq=mmq, ks=kship_gate):
                            vector.wait_ge(sem["smm"], mmq)
                            if ks:
                                vector.wait_ge(sem["sbo"], ks)
                            vector.tensor_copy(t(slice(0, 512)), pw[2 * pp][:])
                            vector.tensor_copy(t(slice(512, 1024)), pw[2 * pp + 1][:]).then_inc(sem["sve"], 1)
                        ev = inc("sve", 1)
                        evac_marks.append(ev)

                        sq0 = inc(sqwsem, 16)
                        sq1 = inc(sqwsem, 16)

                        @block.sync
                        def _(sync, h=h, sl=sl, ev=ev, sqwsem=sqwsem):
                            sync.wait_ge(sem["sve"], ev)
                            sync.dma_start(out=qsw[0:64, sl, :], in_=t(slice(0, TOK))[64:128, :] if False else (ktmp[64:128, :] if is_k else qy[64:128, h, :])).then_inc(sem[sqwsem], 16)
                            sync.dma_start(out=qsw[64:128, sl, :], in_=(ktmp[0:64, :] if is_k else qy[0:64, h, :])).then_inc(sem[sqwsem], 16)

                        @block.vector
                        def _(vector, h=h, ev=ev):
                            vector.tensor_mul(tmpA[:], t(slice(0, 512)), t(slice(0, 512)))
                            vector.tensor_mul(tmpB[:], t(slice(512, 1024)), t(slice(512, 1024))).then_inc(sem["sve"], 1)
                        vsq = inc("sve", 1)

                        @block.tensor
                        def _(tensor, vsq=vsq):
                            tensor.wait_ge(sem["sve"], vsq)
                            tensor.matmul(pr[0][0:1, :], ones_c[:], tmpA[:], start=True, stop=True)
                            tensor.matmul(pr[1][0:1, :], ones_c[:], tmpB[:], start=True, stop=True).then_inc(sem["smm"], 1)
                        mms = inc("smm", 1)

                        _f = 1.0 if is_k else float(DH)

                        ci = 1 if not is_k else 0

                        @block.scalar
                        def _(scalar, mms=mms, ci=ci):
                            scalar.wait_ge(sem["smm"], mms)
                            scalar.activation(row(R_R0), pr[0][0:1, :], AF.Ln,
                                              scale=cst[0:1, 2 + ci:3 + ci], bias=cst[0:1, ci:ci + 1])
                            scalar.activation(row(R_R0), row(R_R0), AF.Exp, scale=-0.5)
                            scalar.activation(row(R_R1), pr[1][0:1, :], AF.Ln,
                                              scale=cst[0:1, 2 + ci:3 + ci], bias=cst[0:1, ci:ci + 1])
                            scalar.activation(row(R_R1), row(R_R1), AF.Exp, scale=-0.5).then_inc(sem["sse"], 1)
                        wr = inc("sse", 1)

                        @block.tensor
                        def _(tensor, wr=wr, rm=rope_marks[-1], cs_ready=cs_ready):
                            tensor.wait_ge(sem["sse"], wr)
                            tensor.wait_ge(sem["sms"], cs_ready)
                            if rm:
                                tensor.wait_ge(sem["sve"], rm)
                            tensor.matmul(pw[4][:], ones_r[:], row(R_R0), start=True, stop=True)
                            tensor.matmul(pw[5][:], ones_r[:], row(R_R1), start=True, stop=True).then_inc(sem["smm"], 1)
                        mmb = inc("smm", 1)

                        @block.vector
                        def _(vector, h=h, sl=sl, mmb=mmb, sq1=sq1, sqwsem=sqwsem):
                            vector.wait_ge(sem["smm"], mmb)
                            vector.wait_ge(sem[sqwsem], sq1)
                            for tc in range(2):
                                ts2 = slice(tc * 512, (tc + 1) * 512)
                                vector.tensor_mul(tb16a[:], t(ts2), cs_sb[:, 0, ts2])
                                vector.tensor_mul(tb16b[:], qsw[:, sl, ts2], cs_sb[:, 1, ts2])
                                vector.tensor_add(tb16a[:], tb16a[:], tb16b[:])
                                if tc == 1:
                                    vector.tensor_mul(t(ts2), tb16a[:], pw[4 + tc][:]).then_inc(sem["sve"], 1)
                                else:
                                    vector.tensor_mul(t(ts2), tb16a[:], pw[4 + tc][:])
                        rope_marks.append(inc("sve", 1))
                        return rope_marks[-1]

                    def run_head_pipeline(heads, conv_base, extra_first=()):
                        n = len(heads)
                        convs = {}
                        mmqs = {}
                        kships = {}
                        for j in range(min(2, n)):
                            convs[j] = head_conv(heads[j][0], heads[j][1], conv_base + j)
                        for i in range(n):
                            h, isk = heads[i]
                            extra = list(extra_first) if i < 2 else []
                            if i >= 2:
                                extra.append(("sve", evac_marks[-1]))
                            mmqs[i] = head_mm(i, h, isk, convs[i][0], convs[i][1], extra)
                            if i + 2 < n:
                                convs[i + 2] = head_conv(heads[i + 2][0], heads[i + 2][1], conv_base + i + 2)
                            if i >= 1:
                                do_chain(i - 1, heads[i - 1], mmqs[i - 1])
                        do_chain(n - 1, heads[n - 1], mmqs[n - 1])

                    def do_chain(i, hk, mmq):
                        h, isk = hk
                        ks = kship_marks[-1] if (isk and kship_marks) else 0
                        mark = head_chain(i, h, isk, mmq, ks)
                        if isk:
                            @block.gpsimd
                            def _(gpsimd, kv=h, mark=mark):
                                gpsimd.wait_ge(sem["sve"], mark)
                                gpsimd.dma_start(
                                    out=cc_in[kv * 128 * TOK:(kv + 1) * 128 * TOK].rearrange("(p m) -> p m", p=128),
                                    in_=ktmp[:]).then_inc(sem["sbo"], 16)
                            kship_marks.append(inc("sbo", 16))

                    evac_marks = [0, 0]
                    kship_marks = []

                    run_head_pipeline([(kv, True) for kv in range(NKV)], 0)

                    mmv_prev = 0
                    for grp in range(2):
                        swc0 = wchunk[0]
                        dma_marks = []
                        for kt in range(NT):
                            sn = "swr%d" % ((swc0 + kt) % 2)
                            dma_marks.append((sn, inc(sn, 16)))

                        @block.sync
                        def _(sync, grp=grp, swc0=swc0, dma_marks=tuple(dma_marks)):
                            for kt in range(NT):
                                n = swc0 + kt
                                if n >= 2:
                                    sync.wait_ge(sem["swc"], n - 1)
                                sync.dma_start(out=wraw[:, n % 2, 0:512], in_=wv_d[kt]).then_inc(
                                    sem[dma_marks[kt][0]], 16)

                        mmv0 = cur("smm")

                        @block.scalar
                        def _(scalar, grp=grp, dma_marks=tuple(dma_marks), swc0=swc0,
                              mmv0=mmv0, mmv_prev=mmv_prev):
                            for kt in range(NT):
                                scalar.wait_ge(sem[dma_marks[kt][0]], dma_marks[kt][1])
                                if kt >= 2:
                                    scalar.wait_ge(sem["smm"], mmv0 + kt - 1)
                                elif grp == 1 and mmv_prev:
                                    scalar.wait_ge(sem["smm"], mmv_prev)
                                scalar.copy(wvb[:, kt % 2, :], wraw[:, (swc0 + kt) % 2, 0:512]).then_inc(sem["swc"], 1)
                        swc_marks = []
                        for kt in range(NT):
                            swc_marks.append(inc("swc", 1))
                        wchunk[0] = swc0 + NT

                        @block.tensor
                        def _(tensor, grp=grp, swc_marks=tuple(swc_marks), rm=rope_marks[-1]):
                            tensor.wait_ge(sem["sve"], ln2_done)
                            if rm:
                                tensor.wait_ge(sem["sve"], rm)
                            for kt in range(NT):
                                tensor.wait_ge(sem["swc"], swc_marks[kt])
                                for tb4 in range(4):
                                    tb = grp * 4 + tb4
                                    mm = tensor.matmul(pw[tb4][:], ln2[:, kt, tb * 128:(tb + 1) * 128],
                                                       wvb[:, kt % 2, :],
                                                       start=(kt == 0), stop=(kt == NT - 1))
                                    if tb4 == 3:
                                        mm.then_inc(sem["smm"], 1)
                        for kt in range(NT):
                            mmv_prev = inc("smm", 1)
                        mmv = cur("smm")

                        bo0 = cur("sbo")

                        @block.scalar
                        def _(scalar, grp=grp, mmv=mmv, bo0=bo0):
                            scalar.wait_ge(sem["smm"], mmv)
                            for tb4 in range(4):
                                scalar.wait_ge(sem["sbo"], bo0 + 16 * tb4)
                                scalar.copy(vtmp[:], pw[tb4][:]).then_inc(sem["sse"], 1)
                        vcp = []
                        for tb4 in range(4):
                            vcp.append(inc("sse", 1))

                        @block.gpsimd
                        def _(gpsimd, grp=grp, vcp=tuple(vcp)):
                            for tb4 in range(4):
                                tb = grp * 4 + tb4
                                gpsimd.wait_ge(sem["sse"], vcp[tb4])
                                off = NKV * 128 * TOK + tb * 128 * 512
                                gpsimd.dma_start(
                                    out=cc_in[off:off + 128 * 512].rearrange("(p m) -> p m", p=128),
                                    in_=vtmp[:]).then_inc(sem["sbo"], 16)
                        for tb4 in range(4):
                            inc("sbo", 16)
                    evac_marks.append(0)
                    vgate = cur("sse")

                    @block.gpsimd
                    def _(gpsimd, w=cur("sbo")):
                        gpsimd.wait_ge(sem["sbo"], w)
                        gpsimd.collective_compute(
                            "AllGather", ALU.bypass, replica_groups=groups,
                            ins=[cc_in[:]], outs=[cc_out[:]]).then_inc(sem["scc"], 1)
                    inc("scc", 1)
                    cc_done = cur("scc")

                    evac_marks = [0, 0]
                    run_head_pipeline([(h, False) for h in range(NH)], NKV,
                                      extra_first=[("sse", vgate)])
                    q_done = cur("sve")

                if debug:
                    @block.gpsimd
                    def _(gpsimd, m=q_done):
                        gpsimd.wait_ge(sem["sve"], m)
                        gpsimd.dma_start(out=dbg["d_q"].rearrange("(t p) m -> p t m", p=128),
                                         in_=qy[:, :, :]).then_inc(sem["sbo"], 16)
                    inc("sbo", 16)

            with nc.sbuf_tensor("kb", [128, NKV, 2048], BF16) as kb, \
                 nc.sbuf_tensor("vb", [128, 16, 512], BF16) as vb, \
                 nc.sbuf_tensor("msks", [128, 8, 512], BF16) as msks, \
                 nc.sbuf_tensor("attb", [128, 6, 512], BF16) as attb, \
                 nc.sbuf_tensor("mscb", [128, 8], F32) as mscb, \
                 nc.sbuf_tensor("mrow", [1, 8], F32) as mrow:

                @block.sync
                def _(sync, cc_done=cc_done, q_done=q_done):
                    sync.wait_ge(sem["scc"], cc_done)
                    sync.wait_ge(sem["sve"], q_done)
                    for kv in range(NKV):
                        sync.dma_start(out=kb[:, kv, 0:1024],
                                       in_=cc_out[kv * 128 * TOK:(kv + 1) * 128 * TOK].rearrange("(p m) -> p m", p=128)).then_inc(sem["sbo"], 16)
                        sync.dma_start(out=kb[:, kv, 1024:2048],
                                       in_=cc_in[kv * 128 * TOK:(kv + 1) * 128 * TOK].rearrange("(p m) -> p m", p=128)).then_inc(sem["sbo"], 16)
                    for tb in range(8):
                        off = NKV * 128 * TOK + tb * 128 * 512
                        sync.dma_start(out=vb[:, tb, :],
                                       in_=cc_out[off:off + 128 * 512].rearrange("(p m) -> p m", p=128)).then_inc(sem["sbo"], 16)
                        sync.dma_start(out=vb[:, 8 + tb, :],
                                       in_=cc_in[off:off + 128 * 512].rearrange("(p m) -> p m", p=128)).then_inc(sem["sbo"], 16)
                inc("sbo", 16 * (2 * NKV + 16))
                kv_ready = cur("sbo")

                @block.scalar
                def _(scalar, q_done=q_done):
                    scalar.wait_ge(sem["sve"], q_done)
                    scalar.dma_start(out=msks[:, :, :],
                                     in_=msk_d.rearrange("a p m -> p a m")).then_inc(sem["sms"], 16)
                    scalar.dma_start(out=mrow[:], in_=mscal_d[:]).then_inc(sem["sms"], 16)
                inc("sms", 32)
                msk_ready = cur("sms")

                @block.tensor
                def _(tensor, msk_ready=msk_ready):
                    tensor.wait_ge(sem["sms"], msk_ready)
                    tensor.matmul(pw[5][:, 0:8], ones_r[:], mrow[:], start=True, stop=True).then_inc(sem["smm"], 1)
                wms = inc("smm", 1)

                @block.vector
                def _(vector, wms=wms):
                    vector.wait_ge(sem["smm"], wms)
                    vector.tensor_copy(mscb[:], pw[5][:, 0:8]).then_inc(sem["sve"], 1)
                mscb_ready = inc("sve", 1)

                if debug:
                    @block.gpsimd
                    def _(gpsimd, kv_ready=kv_ready):
                        gpsimd.wait_ge(sem["sbo"], kv_ready)
                        gpsimd.dma_start(out=dbg["d_kb"].rearrange("(a p) m -> p a m", p=128),
                                         in_=kb[:, :, :]).then_inc(sem["sbo"], 16)
                        gpsimd.dma_start(out=dbg["d_vb"].rearrange("(a p) m -> p a m", p=128),
                                         in_=vb[:, :, :]).then_inc(sem["sbo"], 16)
                    inc("sbo", 32)

                prev_y = [0, 0]
                for h in range(NH):
                    kv = h // 4
                    base_mm = cur("smm")
                    base_se = cur("sse")
                    base_ve = cur("sve")
                    def sc_mark(st, j):
                        return base_mm + (1 + st if j == 0 else 4 * j - 1 + st)

                    def av_mark(st, j):
                        if j == 11:
                            return base_mm + 47 + st
                        return base_mm + 4 * j + 5 + st

                    def exp_mark(st, j):
                        return base_se + 2 * j + st + 1

                    def msk_mark(st, j):
                        return base_ve + 2 * j + st + 1

                    @block.tensor
                    def _(tensor, h=h, kv=kv, base_mm=base_mm, base_se=base_se, base_ve=base_ve,
                          prev_y=tuple(prev_y), kv_ready=kv_ready, q_done=q_done,
                          mscb_ready=mscb_ready, sc_mark=sc_mark, av_mark=av_mark,
                          exp_mark=exp_mark, msk_mark=msk_mark, aser=(cur("sve") if serial else 0)):
                        if aser:
                            tensor.wait_ge(sem["sve"], aser)
                        if h == 0:
                            tensor.wait_ge(sem["sbo"], kv_ready)
                            tensor.wait_ge(sem["sve"], q_done)
                            tensor.wait_ge(sem["sve"], mscb_ready)
                        for j in range(12):
                            for st in range(2):
                                if j < 2 and prev_y[st]:
                                    tensor.wait_ge(sem["sve"], prev_y[st])
                                if j >= 2:
                                    tensor.wait_ge(sem["sse"], exp_mark(st, j - 2))
                                col = st * 512 + j * 128
                                tensor.matmul(pw[2 * st + j % 2][:], kb[:, kv, col:col + 128],
                                              qy[:, h, st * 512:(st + 1) * 512],
                                              start=True, stop=True).then_inc(sem["smm"], 1)
                            for st in range(2):
                                if j >= 1:
                                    jj = j - 1
                                    tensor.wait_ge(sem["sve"], msk_mark(st, jj))
                                    aslot = 3 * st + jj % 3
                                    tensor.matmul(pr[st][0:1, :], ones_cb[:], attb[:, aslot, :],
                                                  start=(jj == 0), stop=False)
                                    tb = st * 4 + jj
                                    mm = tensor.matmul(pw[4 + st][:], vb[:, tb, kv * 128:(kv + 1) * 128],
                                                       attb[:, aslot, :],
                                                       start=(jj == 0), stop=False)
                                    mm.then_inc(sem["smm"], 1)
                        for st in range(2):
                            jj = 11
                            tensor.wait_ge(sem["sve"], msk_mark(st, jj))
                            aslot = 3 * st + jj % 3
                            tensor.matmul(pr[st][0:1, :], ones_cb[:], attb[:, aslot, :],
                                          start=False, stop=True)
                            tb = st * 4 + jj
                            tensor.matmul(pw[4 + st][:], vb[:, tb, kv * 128:(kv + 1) * 128],
                                          attb[:, aslot, :],
                                          start=False, stop=True).then_inc(sem["smm"], 1)
                    inc("smm", 48)

                    @block.scalar
                    def _(scalar, sc_mark=sc_mark, av_mark=av_mark):
                        for j in range(12):
                            for st in range(2):
                                scalar.wait_ge(sem["smm"], sc_mark(st, j))
                                if j >= 3:
                                    scalar.wait_ge(sem["smm"], av_mark(st, j - 3))
                                scalar.activation(attb[:, 3 * st + j % 3, :],
                                                  pw[2 * st + j % 2][:], AF.Exp).then_inc(sem["sse"], 1)
                    inc("sse", 24)
                    prev_exp_end = cur("sse")

                    @block.vector
                    def _(vector, exp_mark=exp_mark):
                        for j in range(12):
                            for st in range(2):
                                vector.wait_ge(sem["sse"], exp_mark(st, j))
                                a = attb[:, 3 * st + j % 3, :]
                                if j < 4:
                                    vector.tensor_mul(a, a, msks[:, j, :]).then_inc(sem["sve"], 1)
                                elif j < 8:
                                    vector.tensor_scalar_mul(a, a, mscb[:, st * 4 + (j - 4):st * 4 + (j - 4) + 1]).then_inc(sem["sve"], 1)
                                else:
                                    vector.tensor_mul(a, a, msks[:, j - 4, :]).then_inc(sem["sve"], 1)
                    inc("sve", 24)

                    for st in range(2):
                        rrow = R_T0 if st == 0 else R_T1

                        @block.scalar
                        def _(scalar, st=st, rrow=rrow, m=av_mark(st, 11)):
                            scalar.wait_ge(sem["smm"], m)
                            scalar.activation(row(rrow), pr[st][0:1, :], AF.Ln)
                            scalar.activation(row(rrow), row(rrow), AF.Exp, scale=-1.0).then_inc(sem["sse"], 1)
                        wr1 = inc("sse", 1)

                        @block.tensor
                        def _(tensor, st=st, rrow=rrow, wr1=wr1, e=exp_mark(st, 11)):
                            tensor.wait_ge(sem["sse"], wr1)
                            tensor.wait_ge(sem["sse"], e)
                            tensor.matmul(pw[2 * st][:], ones_r[:], row(rrow), start=True, stop=True).then_inc(sem["smm"], 1)
                        wr2 = inc("smm", 1)

                        @block.vector
                        def _(vector, h=h, st=st, wr2=wr2):
                            vector.wait_ge(sem["smm"], wr2)
                            vector.tensor_copy(fold[:, st * 512:(st + 1) * 512], pw[2 * st][:])
                            vector.tensor_mul(qy[:, h, st * 512:(st + 1) * 512],
                                              pw[4 + st][:], fold[:, st * 512:(st + 1) * 512]).then_inc(sem["sve"], 1)
                        prev_y[st] = inc("sve", 1)
                y_done = cur("sve")

                if debug:
                    @block.gpsimd
                    def _(gpsimd, m=y_done):
                        gpsimd.wait_ge(sem["sve"], m)
                        gpsimd.dma_start(out=dbg["d_y"].rearrange("(t p) m -> p t m", p=128),
                                         in_=qy[:, :, :]).then_inc(sem["sbo"], 16)
                    inc("sbo", 16)

            pendo = []
            last_ev = 0
            for dt in range(NT):
                par = dt % 3
                w_rdy = stream_strip(wo_d, dt, wbf_a, par, "copy")
                pendo.append((dt, w_rdy, par))

                def compute_wo(dt, w_rdy, par):
                    nonlocal last_ev
                    marks = []

                    @block.tensor
                    def _(tensor, dt=dt, w_rdy=w_rdy, par=par, y_done=y_done, last_ev=last_ev):
                        tensor.wait_ge(sem[w_rdy[0]], w_rdy[1])
                        tensor.wait_ge(sem["sve"], y_done)
                        if last_ev:
                            tensor.wait_ge(sem["sve"], last_ev)
                        for s2 in range(NH):
                            for tc in range(2):
                                mm = tensor.matmul(pw[4 + tc][:], wbf_a[:, par, s2 * 128:(s2 + 1) * 128],
                                                   qy[:, s2, tc * 512:(tc + 1) * 512],
                                                   start=(s2 == 0), stop=(s2 == NH - 1))
                                if s2 == NH - 1:
                                    mm.then_inc(sem["smm"], 1)
                    marks.append(inc("smm", 1))
                    marks.append(inc("smm", 1))
                    buf_free[("wbf_a", par)] = marks[1]

                    @block.vector
                    def _(vector, dt=dt, marks=tuple(marks)):
                        vector.wait_ge(sem["smm"], marks[0])
                        vector.tensor_add(x_sb[:, dt, 0:512], x_sb[:, dt, 0:512], pw[4][:])
                        vector.wait_ge(sem["smm"], marks[1])
                        vector.tensor_add(x_sb[:, dt, 512:1024], x_sb[:, dt, 512:1024], pw[5][:]).then_inc(sem["sve"], 1)
                    last_ev = inc("sve", 1)

                if len(pendo) == 3:
                    compute_wo(*pendo.pop(0))
            while pendo:
                compute_wo(*pendo.pop(0))
        x2_done = cur("sve")

        if debug:
            @block.gpsimd
            def _(gpsimd, m=x2_done):
                gpsimd.wait_ge(sem["sve"], m)
                gpsimd.dma_start(out=dbg["d_x2"].rearrange("(t p) m -> p t m", p=128),
                                 in_=x_sb[:, :, :]).then_inc(sem["sbo"], 16)
            inc("sbo", 16)

        with nc.sbuf_tensor("xq2", [128, NT, TOK], BF16) as xq2, \
             nc.sbuf_tensor("p2", [128, SPQ, TOK], BF16) as p2:
            xq2_ready = emit_lnq(xq2, [("sve", x2_done)])
            emit_ffn_mats(xq2, p2, False, xq2_ready)

        st_fin = cur("sst")

        @block.gpsimd
        def _(gpsimd, st_fin=st_fin):
            gpsimd.wait_ge(sem["sst"], st_fin)

    return nc



def _tile_lhsT(W):
    outF, inF = W.shape
    S, KT = outF // 128, inF // 128
    t = W.reshape(S, 128, KT, 128)
    return np.ascontiguousarray(t.transpose(0, 3, 2, 1).reshape(S, 128, KT * 128))


def _rope_perm():
    p = np.empty(128, np.int64)
    p[:64] = np.arange(0, 128, 2)
    p[64:] = np.arange(1, 128, 2)
    return p


def _host_inputs(L0_attn_x, L0_attn_wq, L0_attn_wk, L0_attn_wv, L0_attn_wo,
                 L0_ffn_W_g, L0_ffn_W_u, L0_ffn_W_d, n_cores=8):
    wg_t = _tile_lhsT(np.asarray(L0_ffn_W_g))
    wu_t = _tile_lhsT(np.asarray(L0_ffn_W_u))
    Wd = np.asarray(L0_ffn_W_d)
    t = Wd.reshape(NT, 128, NQH, SPQ, 128)
    wd_t = np.ascontiguousarray(t.transpose(2, 0, 4, 3, 1).reshape(NSTRIP, 128, 2048))
    perm = _rope_perm()
    wq = np.asarray(L0_attn_wq).reshape(NH, 128, D)[:, perm, :].reshape(NH * 128, D)
    wk = np.asarray(L0_attn_wk).reshape(NKV, 128, D)[:, perm, :].reshape(NKV * 128, D)
    wq_t = _tile_lhsT(wq)
    wk_t = _tile_lhsT(wk)
    wv_t = np.ascontiguousarray(np.asarray(L0_attn_wv).T.reshape(NT, 128, 512))
    wo_t = _tile_lhsT(np.asarray(L0_attn_wo))
    swap = np.eye(128, dtype=np.float32)
    i_idx = np.arange(128)[:, None]
    c_idx = np.arange(512)[None, :]
    m_lo1 = np.stack([(c_idx <= j * 128 + i_idx - 1) for j in range(4)]).astype(ml_dtypes.bfloat16)
    m_lo0 = np.zeros_like(m_lo1)
    m_ca = np.stack([(c_idx >= j * 128 + i_idx) for j in range(4)]).astype(ml_dtypes.bfloat16)
    msk1 = np.ascontiguousarray(np.concatenate([m_lo1, m_ca], 0))
    msk0 = np.ascontiguousarray(np.concatenate([m_lo0, m_ca], 0))
    mscal1 = np.ones((1, 8), np.float32)
    mscal0 = np.array([[0, 0, 0, 0, 1, 1, 1, 1]], np.float32)
    inv = 1.0 / 10000.0 ** (np.arange(0, DH, 2, dtype=np.float64) / DH)
    x = np.asarray(L0_attn_x)
    in_maps = []
    for c in range(n_cores):
        b, hf = c // 2, c % 2
        t_abs = hf * TOK + np.arange(TOK)
        fr = np.outer(t_abs.astype(np.float64), inv)
        cos = np.cos(fr).astype(np.float32).T
        sin = np.sin(fr).astype(np.float32).T
        cs = np.empty((2, 128, TOK), np.float32)
        cs[0, :64] = cos
        cs[0, 64:] = cos
        cs[1, :64] = -sin
        cs[1, 64:] = sin
        xs = np.ascontiguousarray(x[b, hf * TOK:(hf + 1) * TOK, :].T)
        in_maps.append({
            "x": xs, "wg": wg_t, "wu": wu_t, "wd": wd_t,
            "wq": wq_t, "wk": wk_t, "wv": wv_t, "wo": wo_t,
            "msk": (msk1 if hf else msk0), "mscal": (mscal1 if hf else mscal0),
            "cs": cs, "swap": swap,
        })
    return in_maps


_NC_CACHE = {}


def kernel(L0_attn_x, L0_attn_wq, L0_attn_wk, L0_attn_wv, L0_attn_wo,
           L0_ffn_W_g, L0_ffn_W_u, L0_ffn_W_d):
    n_cores = 8
    in_maps = _host_inputs(L0_attn_x, L0_attn_wq, L0_attn_wk, L0_attn_wv,
                           L0_attn_wo, L0_ffn_W_g, L0_ffn_W_u, L0_ffn_W_d, n_cores)
    if n_cores not in _NC_CACHE:
        _NC_CACHE[n_cores] = build_program(n_cores=n_cores, debug=False)
    nc = _NC_CACHE[n_cores]
    res = run_bass_kernel_spmd(nc, in_maps, core_ids=list(range(n_cores)))
    out = np.empty((4, 2048, 2048), np.float32)
    for c in range(n_cores):
        b, hf = c // 2, c % 2
        out[b, hf * TOK:(hf + 1) * TOK, :] = np.asarray(res.results[c]["out"]).T
    return out


# version 20
# speedup vs baseline: 1.1851x; 1.1851x over previous
import numpy as np
import ml_dtypes
from contextlib import ExitStack

import concourse.bass as bass
import concourse.mybir as mybir
import concourse.bass_utils as _bu
from concourse.bass_utils import run_bass_kernel_spmd

if not getattr(_bu, "_ldw_opt_patched", False):
    _orig_run_command = _bu.run_command

    def _run_command_ldw(cmd, *a, **kw):
        try:
            cmd = ["--enable-ldw-opt=true" if c == "--enable-ldw-opt=false" else c for c in cmd]
        except TypeError:
            pass
        return _orig_run_command(cmd, *a, **kw)

    _bu.run_command = _run_command_ldw
    _bu._ldw_opt_patched = True

F32 = mybir.dt.float32
BF16 = mybir.dt.bfloat16
I32 = mybir.dt.int32
AF = mybir.ActivationFunctionType
ALU = mybir.AluOpType
AX = mybir.AxisListType

D = 2048
TOK = 1024
NT = 16
HID = 8192
NH, NKV, DH = 16, 4, 128
NSTRIP = 64
NQH = 4
SPQ = 16
EPS_LN = 1e-5
EPS_RMS_FFN = 1e-8
EPS_RMS_ATTN = 1e-6
SCALE = float(1.0 / np.sqrt(DH))
RD = float(1.0 / D)
RDH = float(1.0 / DH)


dbg_p_holder = []


def build_program(n_cores=8, debug=False, serial=0):
    nc = bass.Bass("TRN2", target_bir_lowering=False)
    groups = [[2 * i, 2 * i + 1] for i in range(n_cores // 2)]

    x_d = nc.dram_tensor("x", [D, TOK], F32, kind="ExternalInput").ap()
    out_d = nc.dram_tensor("out", [D, TOK], F32, kind="ExternalOutput").ap()
    wg_d = nc.dram_tensor("wg", [NSTRIP, 128, 2048], F32, kind="ExternalInput").ap()
    wu_d = nc.dram_tensor("wu", [NSTRIP, 128, 2048], F32, kind="ExternalInput").ap()
    wd_d = nc.dram_tensor("wd", [NSTRIP, 128, 2048], F32, kind="ExternalInput").ap()
    wq_d = nc.dram_tensor("wq", [NH, 128, 2048], F32, kind="ExternalInput").ap()
    wk_d = nc.dram_tensor("wk", [NKV, 128, 2048], F32, kind="ExternalInput").ap()
    wv_d = nc.dram_tensor("wv", [NT, 128, 512], F32, kind="ExternalInput").ap()
    wo_d = nc.dram_tensor("wo", [NT, 128, 2048], F32, kind="ExternalInput").ap()
    msk_d = nc.dram_tensor("msk", [8, 128, 512], BF16, kind="ExternalInput").ap()
    mscal_d = nc.dram_tensor("mscal", [1, 8], F32, kind="ExternalInput").ap()
    cs_d = nc.dram_tensor("cs", [2, 128, TOK], F32, kind="ExternalInput").ap()
    swap_d = nc.dram_tensor("swap", [128, 128], F32, kind="ExternalInput").ap()
    sg_g = nc.dram_tensor("sg_g", [NSTRIP, 128, 2048], BF16).ap()
    sg_u = nc.dram_tensor("sg_u", [NSTRIP, 128, 2048], BF16).ap()
    sg_d2 = nc.dram_tensor("sg_d2", [NSTRIP, 128, 2048], BF16).ap()
    CCN = NKV * 128 * TOK + 8 * 128 * 512
    cc_in = nc.dram_tensor("cc_in", [CCN], BF16).ap()
    cc_out = nc.dram_tensor("cc_out", [2 * CCN], BF16).ap()
    dbg = {}
    if debug:
        for nm, shp, dt in [("d_x1", [D, TOK], F32), ("d_ln2", [D, TOK], BF16),
                            ("d_q", [D, TOK], BF16), ("d_kb", [NKV * 128, 2048], BF16),
                            ("d_vb", [16 * 128, 512], BF16), ("d_y", [D, TOK], BF16),
                            ("d_x2", [D, TOK], F32), ("d_xq", [D, TOK], BF16),
                            ("d_rows", [1, 8, 512], F32), ("d_fold", [128, TOK], F32),
                            ("d_sgg", [NSTRIP, 128, 2048], BF16), ("d_sgd", [NSTRIP, 128, 2048], BF16),
                            ("d_p", [128, SPQ, TOK], BF16)]:
            dbg[nm] = nc.dram_tensor(nm, shp, dt, kind="ExternalOutput").ap()

    cnt = {}

    def inc(s, n=1):
        cnt[s] = cnt.get(s, 0) + n
        return cnt[s]

    def cur(s):
        return cnt.get(s, 0)

    SEMS = (["sx", "sw", "swc", "sscr", "smm", "sve", "sse", "sgp", "scc", "sbo", "sms", "sst", "smx0", "smx1"]
            + ["sxt%d" % i for i in range(4)]
            + ["swr0", "swr1", "sqw0", "sqw1"]
            + ["scr_%s_%d" % (b, p) for b in ("wbf_a", "wbf_b") for p in (0, 1)]
            + ["swb_%s_%d" % (b, p) for b in ("wbf_a", "wbf_b") for p in (0, 1, 2)]
            + ["scr_%s_2" % b for b in ("wbf_a", "wbf_b")])

    with ExitStack() as ES:
        sem = {k: ES.enter_context(nc.semaphore(k)) for k in SEMS}
        x_sb = ES.enter_context(nc.sbuf_tensor("x_sb", [128, NT, TOK], F32))
        rows = ES.enter_context(nc.sbuf_tensor("rows", [1, 8, 512], F32))
        ones_c = ES.enter_context(nc.sbuf_tensor("ones_c", [128, 1], F32))
        ones_r = ES.enter_context(nc.sbuf_tensor("ones_r", [1, 128], F32))
        ones_cb = ES.enter_context(nc.sbuf_tensor("ones_cb", [128, 1], BF16))
        wraw = ES.enter_context(nc.sbuf_tensor("wraw", [128, 2, 1024], F32))
        tmpP = ES.enter_context(nc.sbuf_tensor("tmpP", [128, 512], F32))
        wbf_a = ES.enter_context(nc.sbuf_tensor("wbf_a", [128, 3, 2048], BF16))
        wbf_b = ES.enter_context(nc.sbuf_tensor("wbf_b", [128, 3, 2048], BF16))
        tmpA = ES.enter_context(nc.sbuf_tensor("tmpA", [128, 512], F32))
        tmpB = ES.enter_context(nc.sbuf_tensor("tmpB", [128, 512], F32))
        slu = ES.enter_context(nc.sbuf_tensor("slu", [128, 2, 512], BF16))
        cst = ES.enter_context(nc.sbuf_tensor("cst", [128, 4], F32))
        ident = ES.enter_context(nc.sbuf_tensor("ident", [128, 128], F32))
        qint = ES.enter_context(nc.sbuf_tensor("qint", [128, 512], I32))
        fold = ES.enter_context(nc.sbuf_tensor("fold", [128, TOK], F32))
        block = ES.enter_context(nc.Block())
        pw = [ES.enter_context(nc.psum_tensor("pw%d" % i, [128, 512], F32)) for i in range(6)]
        pr = [ES.enter_context(nc.psum_tensor("pr%d" % i, [128, 512], F32)) for i in range(2)]

        R_S1, R_S2, R_T0, R_T1, R_A, R_B, R_M, R_M0 = range(8)
        R_AS, R_BS = R_S1, R_S2
        R_R0, R_R1 = R_S1, R_S2
        R_M1 = R_M

        def row(i):
            return rows[0:1, i, :]

        wchunk = [0]
        buf_free = {}
        scr_free = {}

        def stream_strip(src, sidx, dst, parity, mode, scratch=None, free=2048):
            key = (dst.name, parity)
            slot_sem = "swb_%s_%d" % (dst.name, parity)
            if mode == "bf16":
                gate = buf_free.get(key, 0)

                @block.sync
                def _(sync, gate=gate, slot_sem=slot_sem):
                    if gate:
                        sync.wait_ge(sem["smm"], gate)
                    sync.dma_start(out=dst[:, parity, 0:free], in_=src[sidx]).then_inc(sem[slot_sem], 16)
                mark = inc(slot_sem, 16)
                return (slot_sem, mark)
            nch = max(1, free // 1024)
            cw = min(free, 1024)
            c0 = wchunk[0]
            dma_marks = []
            for ch in range(nch):
                sn = "swr%d" % ((c0 + ch) % 2)
                dma_marks.append((sn, inc(sn, 16)))

            @block.sync
            def _(sync, c0=c0, dma_marks=tuple(dma_marks), nch=nch, cw=cw):
                for ch in range(nch):
                    n = c0 + ch
                    if n >= 2:
                        sync.wait_ge(sem["swc"], n - 1)
                    sync.dma_start(out=wraw[:, n % 2, 0:cw],
                                   in_=src[sidx, :, ch * cw:(ch + 1) * cw]).then_inc(
                        sem[dma_marks[ch][0]], 16)
            gate_mm = buf_free.get(key, 0)
            gate_scr = scr_free.get(key, 0)

            @block.vector
            def _(vector, c0=c0, dma_marks=tuple(dma_marks), gate_mm=gate_mm, gate_scr=gate_scr,
                  nch=nch, cw=cw, ser_mm=(cur("smm") if serial else 0)):
                for ch in range(nch):
                    vector.wait_ge(sem[dma_marks[ch][0]], dma_marks[ch][1])
                    if ch == 0:
                        if gate_mm:
                            vector.wait_ge(sem["smm"], gate_mm)
                        if gate_scr:
                            vector.wait_ge(sem[gate_scr[0]], gate_scr[1])
                        if ser_mm:
                            vector.wait_ge(sem["smm"], ser_mm)
                    dstap = dst[:, parity, ch * cw:(ch + 1) * cw]
                    srcap = wraw[:, (c0 + ch) % 2, 0:cw]
                    if mode == "sign":
                        vector.tensor_scalar(srcap, srcap, 1e38, 1.0, ALU.mult, ALU.min)
                        vector.tensor_scalar_max(dstap, srcap, -1.0).then_inc(sem["swc"], 1)
                    else:
                        vector.tensor_copy(dstap, srcap).then_inc(sem["swc"], 1)
            for ch in range(nch):
                mark = inc("swc", 1)
            wchunk[0] = c0 + nch
            if scratch is not None:
                scr_sem = "scr_%s_%d" % (dst.name, parity)

                @block.gpsimd
                def _(gpsimd, mark=mark, scr_sem=scr_sem):
                    gpsimd.wait_ge(sem["swc"], mark)
                    gpsimd.dma_start(out=scratch[sidx], in_=dst[:, parity, :]).then_inc(sem[scr_sem], 16)
                scr_free[key] = (scr_sem, inc(scr_sem, 16))
            return ("swc", mark)

        @block.vector
        def _(vector):
            vector.memset(ones_c[:], 1.0)
            vector.memset(ones_cb[:], 1.0)
            vector.memset(cst[:, 0:1], EPS_RMS_ATTN)
            vector.memset(cst[:, 1:2], EPS_RMS_ATTN * DH)
            vector.memset(cst[:, 2:3], RDH)
            vector.memset(cst[:, 3:4], 1.0)
            vector.memset(ones_r[:], 1.0).then_inc(sem["sve"], 1)
        inc("sve", 1)
        ones_ready = cur("sve")

        @block.gpsimd
        def _(gpsimd):
            for kt in range(0, NT, 2):
                gpsimd.dma_start(out=x_sb[:, kt, :],
                                 in_=x_d[kt * 128:(kt + 1) * 128, :]).then_inc(
                    sem["sxt%d" % (kt % 4)], 16)

        @block.sync
        def _(sync):
            sync.dma_start(out=ident[:], in_=swap_d).then_inc(sem["sx"], 16)
            for kt in range(1, NT, 2):
                sync.dma_start(out=x_sb[:, kt, :],
                               in_=x_d[kt * 128:(kt + 1) * 128, :]).then_inc(
                    sem["sxt%d" % (kt % 4)], 16)
        for kt in range(NT):
            inc("sxt%d" % (kt % 4), 16)
        inc("sx", 16)
        x_ready = cur("sx")

        def x_tile_wait(eng, kt):
            eng.wait_ge(sem["sxt%d" % (kt % 4)], 16 * (kt // 4 + 1))

        mx_count = {"smx0": 0, "smx1": 0}


        def emit_stats(extra_waits):
            raise RuntimeError("inlined")

        def emit_lnq(xq_sb, extra_waits):
            xq_marks = []
            ab_bank = {0: (4, 5), 1: (2, 3)}
            vfold = {}
            vcp = {}
            mx_mark = {}
            stats_mm = {}
            w6m = {}

            def stats_and_algebra(tc, gate_extra, gate_tensor=()):
                ts = slice(tc * 512, (tc + 1) * 512)
                se0 = cur("sse")
                mm0 = cur("smm")
                ve_gate = cur("sve")

                sqb = (tmpA[:], tmpB[:], fold[:, ts])

                @block.scalar
                def _(scalar, ts=ts, mm0=mm0, ew=tuple(extra_waits), ge=tuple(gate_extra)):
                    for sname, sval in ew:
                        scalar.wait_ge(sem[sname], sval)
                    scalar.wait_ge(sem["sx"], x_ready)
                    for sname, sval in ge:
                        scalar.wait_ge(sem[sname], sval)
                    for kt in range(NT):
                        x_tile_wait(scalar, kt)
                        if kt >= 3:
                            scalar.wait_ge(sem["smm"], mm0 + kt - 2)
                        scalar.activation(sqb[kt % 3], x_sb[:, kt, ts], AF.Square).then_inc(sem["sse"], 1)
                inc("sse", NT)

                @block.tensor
                def _(tensor, ts=ts, se0=se0, ew=tuple(extra_waits), ge=tuple(gate_extra),
                      gt=tuple(gate_tensor)):
                    for sname, sval in ew:
                        tensor.wait_ge(sem[sname], sval)
                    tensor.wait_ge(sem["sx"], x_ready)
                    tensor.wait_ge(sem["sve"], ones_ready)
                    for sname, sval in ge:
                        tensor.wait_ge(sem[sname], sval)
                    for sname, sval in gt:
                        tensor.wait_ge(sem[sname], sval)
                    for kt in range(NT):
                        x_tile_wait(tensor, kt)
                        tensor.matmul(pr[0][0:1, :], ones_c[:], x_sb[:, kt, ts],
                                      start=(kt == 0), stop=(kt == NT - 1))
                        tensor.wait_ge(sem["sse"], se0 + kt + 1)
                        tensor.matmul(pr[1][0:1, :], ones_c[:], sqb[kt % 3],
                                      start=(kt == 0), stop=(kt == NT - 1)).then_inc(sem["smm"], 1)
                inc("smm", NT)
                mm_stats = cur("smm")
                stats_mm[tc] = mm_stats

                @block.vector
                def _(vector, mm_stats=mm_stats):
                    vector.wait_ge(sem["smm"], mm_stats)
                    vector.tensor_copy(row(R_S1), pr[0][0:1, :])
                    vector.tensor_copy(row(R_S2), pr[1][0:1, :])
                    vector.tensor_scalar_mul(row(R_T0), row(R_S1), RD)
                    vector.tensor_scalar_mul(row(R_T1), row(R_S2), RD)
                    vector.tensor_mul(row(R_A), row(R_T0), row(R_T0))
                    vector.tensor_sub(row(R_M), row(R_T1), row(R_A))
                    vector.tensor_scalar_add(row(R_T1), row(R_M), EPS_LN).then_inc(sem["sve"], 1)
                w1 = inc("sve", 1)

                @block.scalar
                def _(scalar, w1=w1):
                    scalar.wait_ge(sem["sve"], w1)
                    scalar.activation(row(R_T1), row(R_T1), AF.Ln)
                    scalar.activation(row(R_A), row(R_T1), AF.Exp, scale=-0.5).then_inc(sem["sse"], 1)
                w2 = inc("sse", 1)

                @block.vector
                def _(vector, w2=w2):
                    vector.wait_ge(sem["sse"], w2)
                    vector.tensor_mul(row(R_T1), row(R_A), row(R_A))
                    vector.tensor_mul(row(R_T1), row(R_T1), row(R_M))
                    vector.tensor_scalar_add(row(R_B), row(R_T1), EPS_RMS_FFN).then_inc(sem["sve"], 1)
                w3 = inc("sve", 1)

                @block.scalar
                def _(scalar, w3=w3):
                    scalar.wait_ge(sem["sve"], w3)
                    scalar.activation(row(R_B), row(R_B), AF.Ln)
                    scalar.activation(row(R_B), row(R_B), AF.Exp, scale=-0.5).then_inc(sem["sse"], 1)
                w4 = inc("sse", 1)

                @block.vector
                def _(vector, w4=w4):
                    vector.wait_ge(sem["sse"], w4)
                    vector.tensor_mul(row(R_AS), row(R_A), row(R_B))
                    vector.scalar_tensor_tensor(row(R_BS), row(R_T0), -1.0, row(R_AS),
                                                ALU.mult, ALU.mult).then_inc(sem["sve"], 1)
                w5 = inc("sve", 1)

                ba, bb = ab_bank[tc]

                @block.tensor
                def _(tensor, w5=w5, ba=ba, bb=bb):
                    tensor.wait_ge(sem["sve"], w5)
                    tensor.matmul(pw[ba][:], ones_r[:], row(R_AS), start=True, stop=True)
                    tensor.matmul(pw[bb][:], ones_r[:], row(R_BS), start=True, stop=True).then_inc(sem["smm"], 1)
                w6m[tc] = inc("smm", 1)

            def pass_a(tc):
                ts = slice(tc * 512, (tc + 1) * 512)
                ba, bb = ab_bank[tc]

                @block.vector
                def _(vector, ts=ts, w6=w6m[tc], ba=ba, bb=bb):
                    vector.wait_ge(sem["smm"], w6)
                    for kt in range(NT):
                        vector.tensor_mul(tmpP[:], x_sb[:, kt, ts], pw[ba][:])
                        vector.tensor_add(tmpP[:], tmpP[:], pw[bb][:])
                        if kt == 0:
                            vector.scalar_tensor_tensor(fold[:, ts], tmpP[:], -1.0, tmpP[:],
                                                        ALU.mult, ALU.max)
                        else:
                            vector.tensor_max(fold[:, ts], fold[:, ts], tmpP[:])
                            if kt == NT - 1:
                                vector.scalar_tensor_tensor(fold[:, ts], tmpP[:], -1.0, fold[:, ts],
                                                            ALU.mult, ALU.max).then_inc(sem["sve"], 1)
                            else:
                                vector.scalar_tensor_tensor(fold[:, ts], tmpP[:], -1.0, fold[:, ts],
                                                            ALU.mult, ALU.max)
                vfold[tc] = inc("sve", 1)

            def m_chain(tc):
                mrow = R_M0 if tc == 0 else R_M1
                mxsem = "smx0" if tc == 0 else "smx1"

                @block.tensor
                def _(tensor, tc=tc, vf=vfold[tc]):
                    tensor.wait_ge(sem["sve"], vf)
                    for j in range(4):
                        tensor.matmul(pr[tc][:, j * 128:(j + 1) * 128],
                                      fold[:, tc * 512 + j * 128:tc * 512 + (j + 1) * 128],
                                      ident[:], start=True, stop=True).then_inc(sem["smm"], 1)
                ttr = inc("smm", 4)

                f0 = tc * 512
                mc = f0 + 128

                @block.vector
                def _(vector, tc=tc, ttr=ttr, mc=mc):
                    vector.wait_ge(sem["smm"], ttr)
                    for j in range(4):
                        mx = vector.tensor_reduce(fold[:, mc + j:mc + j + 1],
                                                  pr[tc][:, j * 128:(j + 1) * 128], AX.X, ALU.max)
                        if j == 3:
                            mx.then_inc(sem["sve"], 1)
                vmax = inc("sve", 1)

                @block.tensor
                def _(tensor, tc=tc, vmax=vmax, mc=mc):
                    tensor.wait_ge(sem["sve"], vmax)
                    tensor.matmul(pr[tc][0:4, 0:128], fold[:, mc:mc + 4], ident[:],
                                  start=True, stop=True).then_inc(sem["smm"], 1)
                tback = inc("smm", 1)

                @block.vector
                def _(vector, tc=tc, tback=tback, f0=f0):
                    vector.wait_ge(sem["smm"], tback)
                    vector.tensor_copy(fold[0:4, f0:f0 + 128], pr[tc][0:4, 0:128]).then_inc(sem["sve"], 1)
                vcp[tc] = inc("sve", 1)

                @block.gpsimd
                def _(gpsimd, vc=vcp[tc], mrow=mrow, mxsem=mxsem, f0=f0):
                    gpsimd.wait_ge(sem["sve"], vc)
                    for j in range(4):
                        gpsimd.dma_start(out=rows[0:1, mrow, j * 128:(j + 1) * 128],
                                         in_=fold[j:j + 1, f0:f0 + 128]).then_inc(sem[mxsem], 16)
                mx_count[mxsem] += 64
                mx_mark[tc] = (mxsem, mx_count[mxsem])

            def pass_b(tc, prev_b_mark):
                ts = slice(tc * 512, (tc + 1) * 512)
                ba, bb = ab_bank[tc]
                mrow = R_M0 if tc == 0 else R_M1

                s_row = tmpA[0:1, 0:512]
                i_row = tmpB[0:1, 0:512]

                sq_gate = stats_mm.get(1, 0) if tc == 0 else 0

                @block.scalar
                def _(scalar, g=mx_mark[tc], pb=prev_b_mark, mrow=mrow, s_row=s_row,
                      sq_gate=sq_gate):
                    if sq_gate:
                        scalar.wait_ge(sem["smm"], sq_gate)
                    scalar.wait_ge(sem[g[0]], g[1])
                    if pb:
                        scalar.wait_ge(sem["sve"], pb)
                    scalar.activation(s_row, row(mrow), AF.Ln)
                    scalar.activation(s_row, s_row, AF.Exp, scale=-1.0).then_inc(sem["sse"], 1)
                sm = inc("sse", 1)

                @block.vector
                def _(vector, sm=sm, mrow=mrow, s_row=s_row, i_row=i_row, ba=ba, bb=bb):
                    vector.wait_ge(sem["sse"], sm)
                    vector.scalar_tensor_tensor(row(R_T0), pw[ba][0:1, :], 127.0, s_row,
                                                ALU.mult, ALU.mult)
                    vector.scalar_tensor_tensor(row(R_T1), pw[bb][0:1, :], 127.0, s_row,
                                                ALU.mult, ALU.mult)
                    vector.tensor_scalar_mul(i_row, row(mrow), 1.0 / 127.0).then_inc(sem["sve"], 1)
                srow = inc("sve", 1)

                @block.tensor
                def _(tensor, srow=srow, i_row=i_row, ba=ba, bb=bb):
                    tensor.wait_ge(sem["sve"], srow)
                    tensor.matmul(pw[ba][:], ones_r[:], row(R_T0), start=True, stop=True)
                    tensor.matmul(pw[bb][:], ones_r[:], row(R_T1), start=True, stop=True)
                    tensor.matmul(pw[1][:], ones_r[:], i_row, start=True, stop=True).then_inc(sem["smm"], 1)
                sbc = inc("smm", 1)

                @block.vector
                def _(vector, ts=ts, sbc=sbc, ba=ba, bb=bb, xq_sb=xq_sb):
                    vector.wait_ge(sem["smm"], sbc)
                    for kt in range(NT):
                        vector.tensor_mul(tmpP[:], x_sb[:, kt, ts], pw[ba][:])
                        vector.tensor_add(tmpP[:], tmpP[:], pw[bb][:])
                        vector.tensor_copy(qint[:], tmpP[:])
                        if kt == NT - 1:
                            vector.tensor_mul(xq_sb[:, kt, ts], qint[:], pw[1][:]).then_inc(sem["sve"], 1)
                        else:
                            vector.tensor_mul(xq_sb[:, kt, ts], qint[:], pw[1][:])
                xq_marks.append(inc("sve", 1))

            stats_and_algebra(0, [])
            pa_gate = cur("sve")
            pass_a(0)
            stats_and_algebra(1, [("sve", pa_gate)])
            m_chain(0)
            pass_a(1)
            pass_b(0, 0)
            m_chain(1)
            pass_b(1, xq_marks[0])
            return tuple(xq_marks)

        def emit_ffn_mats(xq_sb, p_sb, first_pass, xq_ready_mark):
            mm3_last_q = 0
            for qh in range(NQH):
                pend = []
                pevac = {}
                last_pevac = {0: 0, 1: 0}

                def compute_strip(si, wg_rdy, wu_rdy, par):
                    mmgu = []

                    @block.tensor
                    def _(tensor, si=si, wg_rdy=wg_rdy, wu_rdy=wu_rdy, par=par,
                          lp0=last_pevac[0], lp1=last_pevac[1],
                          serw=(cur("swc") if serial else 0), serv=(cur("sve") if serial else 0)):
                        tensor.wait_ge(sem[wg_rdy[0]], wg_rdy[1])
                        tensor.wait_ge(sem[wu_rdy[0]], wu_rdy[1])
                        if serw:
                            tensor.wait_ge(sem["swc"], serw)
                        if serv:
                            tensor.wait_ge(sem["sve"], serv)
                        tensor.wait_ge(sem["sve"], xq_ready_mark[1])
                        tensor.wait_ge(sem["sve"], lp0)
                        tensor.wait_ge(sem["sve"], lp1)
                        for kt in range(NT):
                            for tc in range(2):
                                mm = tensor.matmul(pw[2 * tc][:], wbf_a[:, par, kt * 128:(kt + 1) * 128],
                                                   xq_sb[:, kt, tc * 512:(tc + 1) * 512],
                                                   start=(kt == 0), stop=(kt == NT - 1))
                            if kt == NT - 1:
                                mm.then_inc(sem["smm"], 1)
                        for kt in range(NT):
                            for tc in range(2):
                                mm = tensor.matmul(pw[2 * tc + 1][:], wbf_b[:, par, kt * 128:(kt + 1) * 128],
                                                   xq_sb[:, kt, tc * 512:(tc + 1) * 512],
                                                   start=(kt == 0), stop=(kt == NT - 1))
                            if kt == NT - 1:
                                mm.then_inc(sem["smm"], 1)
                    mmgu.append(inc("smm", 1))
                    mmgu.append(inc("smm", 1))
                    buf_free[("wbf_a", par)] = mmgu[1]
                    buf_free[("wbf_b", par)] = mmgu[1]

                    @block.scalar
                    def _(scalar, mmgu=tuple(mmgu)):
                        scalar.wait_ge(sem["smm"], mmgu[0])
                        scalar.activation(slu[:, 0, :], pw[0][:], AF.Silu).then_inc(sem["sse"], 1)
                        scalar.wait_ge(sem["smm"], mmgu[1])
                        scalar.activation(slu[:, 1, :], pw[2][:], AF.Silu).then_inc(sem["sse"], 1)
                    silu0 = inc("sse", 1)
                    silu1 = inc("sse", 1)

                    @block.vector
                    def _(vector, si=si, silu0=silu0, silu1=silu1, mm3q=mm3_last_q, p_sb=p_sb, mmgu2=tuple(mmgu)):
                        if mm3q:
                            vector.wait_ge(sem["smm"], mm3q)
                        vector.wait_ge(sem["sse"], silu0)
                        vector.wait_ge(sem["smm"], mmgu2[1])
                        vector.tensor_mul(p_sb[:, si, 0:512], slu[:, 0, :], pw[1][:]).then_inc(sem["sve"], 1)
                        vector.wait_ge(sem["sse"], silu1)
                        vector.tensor_mul(p_sb[:, si, 512:1024], slu[:, 1, :], pw[3][:]).then_inc(sem["sve"], 1)
                    last_pevac[0] = inc("sve", 1)
                    last_pevac[1] = inc("sve", 1)

                for si in range(SPQ):
                    s = qh * SPQ + si
                    par = si % 3
                    if first_pass:
                        wg_rdy = stream_strip(wg_d, s, wbf_a, par, "sign", scratch=sg_g)
                        wu_rdy = stream_strip(wu_d, s, wbf_b, par, "sign", scratch=sg_u)
                    else:
                        wg_rdy = stream_strip(sg_g, s, wbf_a, par, "bf16")
                        wu_rdy = stream_strip(sg_u, s, wbf_b, par, "bf16")
                    pend.append((si, wg_rdy, wu_rdy, par))
                    if len(pend) == 3:
                        compute_strip(*pend.pop(0))
                while pend:
                    compute_strip(*pend.pop(0))
                p_ready = cur("sve")

                pendd = []
                last_ev = 0
                for dt in range(NT):
                    s = qh * SPQ + dt
                    par = dt % 3
                    if first_pass:
                        wd_rdy = stream_strip(wd_d, s, wbf_a, par, "sign", scratch=sg_d2)
                    else:
                        wd_rdy = stream_strip(sg_d2, s, wbf_a, par, "bf16")
                    pendd.append((dt, wd_rdy, par))

                    def compute_dt(dt, wd_rdy, par):
                        nonlocal last_ev
                        mm3 = []

                        @block.tensor
                        def _(tensor, dt=dt, wd_rdy=wd_rdy, par=par, p_ready=p_ready, last_ev=last_ev, p_sb=p_sb,
                              serw=(cur("swc") if serial else 0), serv=(cur("sve") if serial else 0)):
                            tensor.wait_ge(sem[wd_rdy[0]], wd_rdy[1])
                            if serw:
                                tensor.wait_ge(sem["swc"], serw)
                            if serv:
                                tensor.wait_ge(sem["sve"], serv)
                            tensor.wait_ge(sem["sve"], p_ready)
                            if last_ev:
                                tensor.wait_ge(sem["sve"], last_ev)
                            for si2 in range(SPQ):
                                for tc in range(2):
                                    mm = tensor.matmul(pw[4 + tc][:], wbf_a[:, par, si2 * 128:(si2 + 1) * 128],
                                                       p_sb[:, si2, tc * 512:(tc + 1) * 512],
                                                       start=(si2 == 0), stop=(si2 == SPQ - 1))
                                    if si2 == SPQ - 1:
                                        mm.then_inc(sem["smm"], 1)
                        mm3.append(inc("smm", 1))
                        mm3.append(inc("smm", 1))
                        buf_free[("wbf_a", par)] = mm3[1]

                        @block.vector
                        def _(vector, dt=dt, mm3=tuple(mm3)):
                            vector.wait_ge(sem["smm"], mm3[0])
                            vector.scalar_tensor_tensor(x_sb[:, dt, 0:512], pw[4][:], 0.5,
                                                        x_sb[:, dt, 0:512], ALU.mult, ALU.add)
                            vector.wait_ge(sem["smm"], mm3[1])
                            vector.scalar_tensor_tensor(x_sb[:, dt, 512:1024], pw[5][:], 0.5,
                                                        x_sb[:, dt, 512:1024], ALU.mult, ALU.add).then_inc(sem["sve"], 1)
                        last_ev = inc("sve", 1)

                        if not first_pass and qh == NQH - 1:
                            @block.gpsimd
                            def _(gpsimd, dt=dt, ev=last_ev):
                                gpsimd.wait_ge(sem["sve"], ev)
                                gpsimd.dma_start(out=out_d[dt * 128:(dt + 1) * 128, :],
                                                 in_=x_sb[:, dt, :]).then_inc(sem["sst"], 16)
                            inc("sst", 16)

                    if len(pendd) == 3:
                        compute_dt(*pendd.pop(0))
                while pendd:
                    compute_dt(*pendd.pop(0))
                mm3_last_q = cur("smm")
            return cur("sve")

        with nc.sbuf_tensor("xq1", [128, NT, TOK], BF16) as xq1, \
             nc.sbuf_tensor("p1", [128, SPQ, TOK], BF16) as p1:
            xq1_ready = emit_lnq(xq1, [])
            dbg_p_holder.clear()
            dbg_p_holder.append(p1)
            if debug:
                @block.gpsimd
                def _(gpsimd, m=xq1_ready[1]):
                    gpsimd.wait_ge(sem["sve"], m)
                    gpsimd.dma_start(out=dbg["d_xq"].rearrange("(t p) m -> p t m", p=128),
                                     in_=xq1[:, :, :]).then_inc(sem["sbo"], 16)
                    gpsimd.dma_start(out=dbg["d_rows"], in_=rows[:, :, :]).then_inc(sem["sbo"], 16)
                    gpsimd.dma_start(out=dbg["d_fold"], in_=fold[:, :]).then_inc(sem["sbo"], 16)
                inc("sbo", 48)
            ffn1_done = emit_ffn_mats(xq1, p1, True, xq1_ready)
            if debug:
                @block.gpsimd
                def _(gpsimd, m=ffn1_done):
                    gpsimd.wait_ge(sem["sve"], m)
                    gpsimd.dma_start(out=dbg["d_p"], in_=p1[:, :, :]).then_inc(sem["sbo"], 16)
                inc("sbo", 16)

        if debug:
            @block.gpsimd
            def _(gpsimd, m=ffn1_done, scr=cur("sscr")):
                gpsimd.wait_ge(sem["sve"], m)
                gpsimd.wait_ge(sem["sscr"], scr)
                gpsimd.dma_start(out=dbg["d_x1"].rearrange("(t p) m -> p t m", p=128),
                                 in_=x_sb[:, :, :]).then_inc(sem["sbo"], 16)
                gpsimd.dma_start(out=dbg["d_sgg"], in_=sg_g[:, :, :]).then_inc(sem["sbo"], 16)
                gpsimd.dma_start(out=dbg["d_sgd"], in_=sg_d2[:, :, :]).then_inc(sem["sbo"], 16)
            inc("sbo", 48)

        with nc.sbuf_tensor("qy", [128, NH, TOK], BF16) as qy:
            rope_marks = [0]
            proj_ev = [0]
            with nc.sbuf_tensor("ln2", [128, NT, TOK], BF16) as ln2:
                ln2_ab = {0: (4, 5), 1: (2, 3)}
                for tc in range(2):
                    ts = slice(tc * 512, (tc + 1) * 512)
                    se0 = cur("sse")
                    mm0 = cur("smm")
                    ve_gate = cur("sve")

                    sqb = (tmpA[:], tmpB[:], fold[:, ts])

                    @block.scalar
                    def _(scalar, ts=ts, mm0=mm0, fd=ffn1_done, ve_gate=ve_gate):
                        scalar.wait_ge(sem["sve"], fd)
                        scalar.wait_ge(sem["sve"], ve_gate)
                        for kt in range(NT):
                            if kt >= 3:
                                scalar.wait_ge(sem["smm"], mm0 + kt - 2)
                            scalar.activation(sqb[kt % 3], x_sb[:, kt, ts], AF.Square).then_inc(sem["sse"], 1)
                    inc("sse", NT)

                    @block.tensor
                    def _(tensor, ts=ts, se0=se0, fd=ffn1_done, ve_gate=ve_gate):
                        tensor.wait_ge(sem["sve"], fd)
                        tensor.wait_ge(sem["sve"], ve_gate)
                        for kt in range(NT):
                            tensor.matmul(pr[0][0:1, :], ones_c[:], x_sb[:, kt, ts],
                                          start=(kt == 0), stop=(kt == NT - 1))
                            tensor.wait_ge(sem["sse"], se0 + kt + 1)
                            tensor.matmul(pr[1][0:1, :], ones_c[:], sqb[kt % 3],
                                          start=(kt == 0), stop=(kt == NT - 1)).then_inc(sem["smm"], 1)
                    inc("smm", NT)
                    mm_stats = cur("smm")

                    @block.vector
                    def _(vector, mm_stats=mm_stats):
                        vector.wait_ge(sem["smm"], mm_stats)
                        vector.tensor_copy(row(R_S1), pr[0][0:1, :])
                        vector.tensor_copy(row(R_S2), pr[1][0:1, :])
                        vector.tensor_scalar_mul(row(R_T0), row(R_S1), RD)
                        vector.tensor_scalar_mul(row(R_T1), row(R_S2), RD)
                        vector.tensor_mul(row(R_A), row(R_T0), row(R_T0))
                        vector.tensor_sub(row(R_T1), row(R_T1), row(R_A))
                        vector.tensor_scalar_add(row(R_T1), row(R_T1), EPS_LN).then_inc(sem["sve"], 1)
                    w1 = inc("sve", 1)

                    @block.scalar
                    def _(scalar, w1=w1):
                        scalar.wait_ge(sem["sve"], w1)
                        scalar.activation(row(R_T1), row(R_T1), AF.Ln)
                        scalar.activation(row(R_A), row(R_T1), AF.Exp, scale=-0.5).then_inc(sem["sse"], 1)
                    w2 = inc("sse", 1)

                    @block.vector
                    def _(vector, w2=w2):
                        vector.wait_ge(sem["sse"], w2)
                        vector.scalar_tensor_tensor(row(R_B), row(R_T0), -1.0, row(R_A),
                                                    ALU.mult, ALU.mult).then_inc(sem["sve"], 1)
                    w3 = inc("sve", 1)

                    ba2, bb2 = ln2_ab[tc]

                    @block.tensor
                    def _(tensor, w3=w3, ba2=ba2, bb2=bb2):
                        tensor.wait_ge(sem["sve"], w3)
                        tensor.matmul(pw[ba2][:], ones_r[:], row(R_A), start=True, stop=True)
                        tensor.matmul(pw[bb2][:], ones_r[:], row(R_B), start=True, stop=True).then_inc(sem["smm"], 1)
                    w4 = inc("smm", 1)

                    @block.vector
                    def _(vector, ts=ts, w4=w4, ba2=ba2, bb2=bb2):
                        vector.wait_ge(sem["smm"], w4)
                        for kt in range(NT):
                            vector.tensor_mul(tmpP[:], x_sb[:, kt, ts], pw[ba2][:])
                            if kt == NT - 1:
                                vector.tensor_add(ln2[:, kt, ts], tmpP[:], pw[bb2][:]).then_inc(sem["sve"], 1)
                            else:
                                vector.tensor_add(ln2[:, kt, ts], tmpP[:], pw[bb2][:])
                    inc("sve", 1)
                ln2_done = cur("sve")

                if debug:
                    @block.gpsimd
                    def _(gpsimd, m=ln2_done):
                        gpsimd.wait_ge(sem["sve"], m)
                        gpsimd.dma_start(out=dbg["d_ln2"].rearrange("(t p) m -> p t m", p=128),
                                         in_=ln2[:, :, :]).then_inc(sem["sbo"], 16)
                    inc("sbo", 16)

                with nc.sbuf_tensor("cs_sb", [128, 2, TOK], BF16) as cs_sb, \
                     nc.sbuf_tensor("ktmp", [128, TOK], BF16) as ktmp, \
                     nc.sbuf_tensor("qsw", [128, 2, TOK], BF16) as qsw, \
                     nc.sbuf_tensor("tb16a", [128, 512], BF16) as tb16a, \
                     nc.sbuf_tensor("tb16b", [128, 512], BF16) as tb16b, \
                     nc.sbuf_tensor("vtmp", [128, 512], BF16) as vtmp, \
                     nc.sbuf_tensor("wvb", [128, 2, 512], BF16) as wvb:

                    @block.gpsimd
                    def _(gpsimd):
                        gpsimd.dma_start(out=cs_sb[:, :, :],
                                         in_=cs_d.rearrange("a p m -> p a m")).then_inc(sem["sms"], 16)
                    inc("sms", 16)
                    cs_ready = cur("sms")

                    def head_conv(h, is_k, ci):
                        par = ci % 3
                        return par, stream_strip(wk_d if is_k else wq_d, h, wbf_a, par, "copy")

                    def head_mm(hi, h, is_k, par, w_rdy, extra=()):
                        pp = hi % 2

                        @block.tensor
                        def _(tensor, h=h, par=par, pp=pp, w_rdy=w_rdy, extra=tuple(extra)):
                            tensor.wait_ge(sem[w_rdy[0]], w_rdy[1])
                            tensor.wait_ge(sem["sve"], ln2_done)
                            for sn, sv in extra:
                                tensor.wait_ge(sem[sn], sv)
                            for kt in range(NT):
                                for tc in range(2):
                                    mm = tensor.matmul(pw[2 * pp + tc][:],
                                                       wbf_a[:, par, kt * 128:(kt + 1) * 128],
                                                       ln2[:, kt, tc * 512:(tc + 1) * 512],
                                                       start=(kt == 0), stop=(kt == NT - 1))
                                    if kt == NT - 1 and tc == 1:
                                        mm.then_inc(sem["smm"], 1)
                        mmq = inc("smm", 1)
                        buf_free[("wbf_a", par)] = mmq
                        return mmq

                    def head_chain(hi, h, is_k, mmq, kship_gate=0):
                        pp = hi % 2
                        sl = hi % 2
                        sqwsem = "sqw%d" % sl

                        def t(ts2):
                            return ktmp[:, ts2] if is_k else qy[:, h, ts2]

                        @block.vector
                        def _(vector, h=h, pp=pp, mmq=mmq, ks=kship_gate):
                            vector.wait_ge(sem["smm"], mmq)
                            if ks:
                                vector.wait_ge(sem["sbo"], ks)
                            vector.tensor_copy(t(slice(0, 512)), pw[2 * pp][:])
                            vector.tensor_copy(t(slice(512, 1024)), pw[2 * pp + 1][:]).then_inc(sem["sve"], 1)
                        ev = inc("sve", 1)
                        evac_marks.append(ev)

                        sq0 = inc(sqwsem, 16)
                        sq1 = inc(sqwsem, 16)

                        @block.sync
                        def _(sync, h=h, sl=sl, ev=ev, sqwsem=sqwsem):
                            sync.wait_ge(sem["sve"], ev)
                            sync.dma_start(out=qsw[0:64, sl, :], in_=t(slice(0, TOK))[64:128, :] if False else (ktmp[64:128, :] if is_k else qy[64:128, h, :])).then_inc(sem[sqwsem], 16)
                            sync.dma_start(out=qsw[64:128, sl, :], in_=(ktmp[0:64, :] if is_k else qy[0:64, h, :])).then_inc(sem[sqwsem], 16)

                        @block.vector
                        def _(vector, h=h, ev=ev):
                            vector.tensor_mul(tmpA[:], t(slice(0, 512)), t(slice(0, 512)))
                            vector.tensor_mul(tmpB[:], t(slice(512, 1024)), t(slice(512, 1024))).then_inc(sem["sve"], 1)
                        vsq = inc("sve", 1)

                        @block.tensor
                        def _(tensor, vsq=vsq):
                            tensor.wait_ge(sem["sve"], vsq)
                            tensor.matmul(pr[0][0:1, :], ones_c[:], tmpA[:], start=True, stop=True)
                            tensor.matmul(pr[1][0:1, :], ones_c[:], tmpB[:], start=True, stop=True).then_inc(sem["smm"], 1)
                        mms = inc("smm", 1)

                        _f = 1.0 if is_k else float(DH)

                        ci = 1 if not is_k else 0

                        @block.scalar
                        def _(scalar, mms=mms, ci=ci):
                            scalar.wait_ge(sem["smm"], mms)
                            scalar.activation(row(R_R0), pr[0][0:1, :], AF.Ln,
                                              scale=cst[0:1, 2 + ci:3 + ci], bias=cst[0:1, ci:ci + 1])
                            scalar.activation(row(R_R0), row(R_R0), AF.Exp, scale=-0.5)
                            scalar.activation(row(R_R1), pr[1][0:1, :], AF.Ln,
                                              scale=cst[0:1, 2 + ci:3 + ci], bias=cst[0:1, ci:ci + 1])
                            scalar.activation(row(R_R1), row(R_R1), AF.Exp, scale=-0.5).then_inc(sem["sse"], 1)
                        wr = inc("sse", 1)

                        @block.tensor
                        def _(tensor, wr=wr, rm=rope_marks[-1], cs_ready=cs_ready):
                            tensor.wait_ge(sem["sse"], wr)
                            tensor.wait_ge(sem["sms"], cs_ready)
                            if rm:
                                tensor.wait_ge(sem["sve"], rm)
                            tensor.matmul(pw[4][:], ones_r[:], row(R_R0), start=True, stop=True)
                            tensor.matmul(pw[5][:], ones_r[:], row(R_R1), start=True, stop=True).then_inc(sem["smm"], 1)
                        mmb = inc("smm", 1)

                        @block.vector
                        def _(vector, h=h, sl=sl, mmb=mmb, sq1=sq1, sqwsem=sqwsem):
                            vector.wait_ge(sem["smm"], mmb)
                            vector.wait_ge(sem[sqwsem], sq1)
                            for tc in range(2):
                                ts2 = slice(tc * 512, (tc + 1) * 512)
                                vector.tensor_mul(tb16a[:], t(ts2), cs_sb[:, 0, ts2])
                                vector.tensor_mul(tb16b[:], qsw[:, sl, ts2], cs_sb[:, 1, ts2])
                                vector.tensor_add(tb16a[:], tb16a[:], tb16b[:])
                                if tc == 1:
                                    vector.tensor_mul(t(ts2), tb16a[:], pw[4 + tc][:]).then_inc(sem["sve"], 1)
                                else:
                                    vector.tensor_mul(t(ts2), tb16a[:], pw[4 + tc][:])
                        rope_marks.append(inc("sve", 1))
                        return rope_marks[-1]

                    def run_head_pipeline(heads, conv_base, extra_first=()):
                        n = len(heads)
                        convs = {}
                        mmqs = {}
                        kships = {}
                        for j in range(min(2, n)):
                            convs[j] = head_conv(heads[j][0], heads[j][1], conv_base + j)
                        for i in range(n):
                            h, isk = heads[i]
                            extra = list(extra_first) if i < 2 else []
                            if i >= 2:
                                extra.append(("sve", evac_marks[-1]))
                            mmqs[i] = head_mm(i, h, isk, convs[i][0], convs[i][1], extra)
                            if i + 2 < n:
                                convs[i + 2] = head_conv(heads[i + 2][0], heads[i + 2][1], conv_base + i + 2)
                            if i >= 1:
                                do_chain(i - 1, heads[i - 1], mmqs[i - 1])
                        do_chain(n - 1, heads[n - 1], mmqs[n - 1])

                    def do_chain(i, hk, mmq):
                        h, isk = hk
                        ks = kship_marks[-1] if (isk and kship_marks) else 0
                        mark = head_chain(i, h, isk, mmq, ks)
                        if isk:
                            @block.gpsimd
                            def _(gpsimd, kv=h, mark=mark):
                                gpsimd.wait_ge(sem["sve"], mark)
                                gpsimd.dma_start(
                                    out=cc_in[kv * 128 * TOK:(kv + 1) * 128 * TOK].rearrange("(p m) -> p m", p=128),
                                    in_=ktmp[:]).then_inc(sem["sbo"], 16)
                            kship_marks.append(inc("sbo", 16))

                    evac_marks = [0, 0]
                    kship_marks = []

                    run_head_pipeline([(kv, True) for kv in range(NKV)], 0)

                    mmv_prev = 0
                    for grp in range(2):
                        swc0 = wchunk[0]
                        dma_marks = []
                        for kt in range(NT):
                            sn = "swr%d" % ((swc0 + kt) % 2)
                            dma_marks.append((sn, inc(sn, 16)))

                        @block.sync
                        def _(sync, grp=grp, swc0=swc0, dma_marks=tuple(dma_marks)):
                            for kt in range(NT):
                                n = swc0 + kt
                                if n >= 2:
                                    sync.wait_ge(sem["swc"], n - 1)
                                sync.dma_start(out=wraw[:, n % 2, 0:512], in_=wv_d[kt]).then_inc(
                                    sem[dma_marks[kt][0]], 16)

                        mmv0 = cur("smm")

                        @block.scalar
                        def _(scalar, grp=grp, dma_marks=tuple(dma_marks), swc0=swc0,
                              mmv0=mmv0, mmv_prev=mmv_prev):
                            for kt in range(NT):
                                scalar.wait_ge(sem[dma_marks[kt][0]], dma_marks[kt][1])
                                if kt >= 2:
                                    scalar.wait_ge(sem["smm"], mmv0 + kt - 1)
                                elif grp == 1 and mmv_prev:
                                    scalar.wait_ge(sem["smm"], mmv_prev)
                                scalar.copy(wvb[:, kt % 2, :], wraw[:, (swc0 + kt) % 2, 0:512]).then_inc(sem["swc"], 1)
                        swc_marks = []
                        for kt in range(NT):
                            swc_marks.append(inc("swc", 1))
                        wchunk[0] = swc0 + NT

                        @block.tensor
                        def _(tensor, grp=grp, swc_marks=tuple(swc_marks), rm=rope_marks[-1]):
                            tensor.wait_ge(sem["sve"], ln2_done)
                            if rm:
                                tensor.wait_ge(sem["sve"], rm)
                            for kt in range(NT):
                                tensor.wait_ge(sem["swc"], swc_marks[kt])
                                for tb4 in range(4):
                                    tb = grp * 4 + tb4
                                    mm = tensor.matmul(pw[tb4][:], ln2[:, kt, tb * 128:(tb + 1) * 128],
                                                       wvb[:, kt % 2, :],
                                                       start=(kt == 0), stop=(kt == NT - 1))
                                    if tb4 == 3:
                                        mm.then_inc(sem["smm"], 1)
                        for kt in range(NT):
                            mmv_prev = inc("smm", 1)
                        mmv = cur("smm")

                        bo0 = cur("sbo")

                        @block.scalar
                        def _(scalar, grp=grp, mmv=mmv, bo0=bo0):
                            scalar.wait_ge(sem["smm"], mmv)
                            for tb4 in range(4):
                                scalar.wait_ge(sem["sbo"], bo0 + 16 * tb4)
                                scalar.copy(vtmp[:], pw[tb4][:]).then_inc(sem["sse"], 1)
                        vcp = []
                        for tb4 in range(4):
                            vcp.append(inc("sse", 1))

                        @block.gpsimd
                        def _(gpsimd, grp=grp, vcp=tuple(vcp)):
                            for tb4 in range(4):
                                tb = grp * 4 + tb4
                                gpsimd.wait_ge(sem["sse"], vcp[tb4])
                                off = NKV * 128 * TOK + tb * 128 * 512
                                gpsimd.dma_start(
                                    out=cc_in[off:off + 128 * 512].rearrange("(p m) -> p m", p=128),
                                    in_=vtmp[:]).then_inc(sem["sbo"], 16)
                        for tb4 in range(4):
                            inc("sbo", 16)
                    evac_marks.append(0)
                    vgate = cur("sse")

                    @block.gpsimd
                    def _(gpsimd, w=cur("sbo")):
                        gpsimd.wait_ge(sem["sbo"], w)
                        gpsimd.collective_compute(
                            "AllGather", ALU.bypass, replica_groups=groups,
                            ins=[cc_in[:]], outs=[cc_out[:]]).then_inc(sem["scc"], 1)
                    inc("scc", 1)
                    cc_done = cur("scc")

                    evac_marks = [0, 0]
                    run_head_pipeline([(h, False) for h in range(NH)], NKV,
                                      extra_first=[("sse", vgate)])
                    q_done = cur("sve")

                if debug:
                    @block.gpsimd
                    def _(gpsimd, m=q_done):
                        gpsimd.wait_ge(sem["sve"], m)
                        gpsimd.dma_start(out=dbg["d_q"].rearrange("(t p) m -> p t m", p=128),
                                         in_=qy[:, :, :]).then_inc(sem["sbo"], 16)
                    inc("sbo", 16)

            with nc.sbuf_tensor("kb", [128, NKV, 2048], BF16) as kb, \
                 nc.sbuf_tensor("vb", [128, 16, 512], BF16) as vb, \
                 nc.sbuf_tensor("msks", [128, 8, 512], BF16) as msks, \
                 nc.sbuf_tensor("attb", [128, 6, 512], BF16) as attb, \
                 nc.sbuf_tensor("mscb", [128, 8], F32) as mscb, \
                 nc.sbuf_tensor("mrow", [1, 8], F32) as mrow:

                @block.sync
                def _(sync, cc_done=cc_done, q_done=q_done):
                    sync.wait_ge(sem["scc"], cc_done)
                    sync.wait_ge(sem["sve"], q_done)
                    for kv in range(NKV):
                        sync.dma_start(out=kb[:, kv, 0:1024],
                                       in_=cc_out[kv * 128 * TOK:(kv + 1) * 128 * TOK].rearrange("(p m) -> p m", p=128)).then_inc(sem["sbo"], 16)
                        sync.dma_start(out=kb[:, kv, 1024:2048],
                                       in_=cc_in[kv * 128 * TOK:(kv + 1) * 128 * TOK].rearrange("(p m) -> p m", p=128)).then_inc(sem["sbo"], 16)
                    for tb in range(8):
                        off = NKV * 128 * TOK + tb * 128 * 512
                        sync.dma_start(out=vb[:, tb, :],
                                       in_=cc_out[off:off + 128 * 512].rearrange("(p m) -> p m", p=128)).then_inc(sem["sbo"], 16)
                        sync.dma_start(out=vb[:, 8 + tb, :],
                                       in_=cc_in[off:off + 128 * 512].rearrange("(p m) -> p m", p=128)).then_inc(sem["sbo"], 16)
                inc("sbo", 16 * (2 * NKV + 16))
                kv_ready = cur("sbo")

                @block.scalar
                def _(scalar, q_done=q_done):
                    scalar.wait_ge(sem["sve"], q_done)
                    scalar.dma_start(out=msks[:, :, :],
                                     in_=msk_d.rearrange("a p m -> p a m")).then_inc(sem["sms"], 16)
                    scalar.dma_start(out=mrow[:], in_=mscal_d[:]).then_inc(sem["sms"], 16)
                inc("sms", 32)
                msk_ready = cur("sms")

                @block.tensor
                def _(tensor, msk_ready=msk_ready):
                    tensor.wait_ge(sem["sms"], msk_ready)
                    tensor.matmul(pw[5][:, 0:8], ones_r[:], mrow[:], start=True, stop=True).then_inc(sem["smm"], 1)
                wms = inc("smm", 1)

                @block.vector
                def _(vector, wms=wms):
                    vector.wait_ge(sem["smm"], wms)
                    vector.tensor_copy(mscb[:], pw[5][:, 0:8]).then_inc(sem["sve"], 1)
                mscb_ready = inc("sve", 1)

                if debug:
                    @block.gpsimd
                    def _(gpsimd, kv_ready=kv_ready):
                        gpsimd.wait_ge(sem["sbo"], kv_ready)
                        gpsimd.dma_start(out=dbg["d_kb"].rearrange("(a p) m -> p a m", p=128),
                                         in_=kb[:, :, :]).then_inc(sem["sbo"], 16)
                        gpsimd.dma_start(out=dbg["d_vb"].rearrange("(a p) m -> p a m", p=128),
                                         in_=vb[:, :, :]).then_inc(sem["sbo"], 16)
                    inc("sbo", 32)

                prev_y = [0, 0]
                for h in range(NH):
                    kv = h // 4
                    base_mm = cur("smm")
                    base_se = cur("sse")
                    base_ve = cur("sve")
                    def sc_mark(st, j):
                        return base_mm + (1 + st if j == 0 else 4 * j - 1 + st)

                    def av_mark(st, j):
                        if j == 11:
                            return base_mm + 47 + st
                        return base_mm + 4 * j + 5 + st

                    def exp_mark(st, j):
                        return base_se + 2 * j + st + 1

                    def msk_mark(st, j):
                        return base_ve + 2 * j + st + 1

                    @block.tensor
                    def _(tensor, h=h, kv=kv, base_mm=base_mm, base_se=base_se, base_ve=base_ve,
                          prev_y=tuple(prev_y), kv_ready=kv_ready, q_done=q_done,
                          mscb_ready=mscb_ready, sc_mark=sc_mark, av_mark=av_mark,
                          exp_mark=exp_mark, msk_mark=msk_mark, aser=(cur("sve") if serial else 0)):
                        if aser:
                            tensor.wait_ge(sem["sve"], aser)
                        if h == 0:
                            tensor.wait_ge(sem["sbo"], kv_ready)
                            tensor.wait_ge(sem["sve"], q_done)
                            tensor.wait_ge(sem["sve"], mscb_ready)
                        for j in range(12):
                            for st in range(2):
                                if j < 2 and prev_y[st]:
                                    tensor.wait_ge(sem["sve"], prev_y[st])
                                if j >= 2:
                                    tensor.wait_ge(sem["sse"], exp_mark(st, j - 2))
                                col = st * 512 + j * 128
                                tensor.matmul(pw[2 * st + j % 2][:], kb[:, kv, col:col + 128],
                                              qy[:, h, st * 512:(st + 1) * 512],
                                              start=True, stop=True).then_inc(sem["smm"], 1)
                            for st in range(2):
                                if j >= 1:
                                    jj = j - 1
                                    tensor.wait_ge(sem["sve"], msk_mark(st, jj))
                                    aslot = 3 * st + jj % 3
                                    tensor.matmul(pr[st][0:1, :], ones_cb[:], attb[:, aslot, :],
                                                  start=(jj == 0), stop=False)
                                    tb = st * 4 + jj
                                    mm = tensor.matmul(pw[4 + st][:], vb[:, tb, kv * 128:(kv + 1) * 128],
                                                       attb[:, aslot, :],
                                                       start=(jj == 0), stop=False)
                                    mm.then_inc(sem["smm"], 1)
                        for st in range(2):
                            jj = 11
                            tensor.wait_ge(sem["sve"], msk_mark(st, jj))
                            aslot = 3 * st + jj % 3
                            tensor.matmul(pr[st][0:1, :], ones_cb[:], attb[:, aslot, :],
                                          start=False, stop=True)
                            tb = st * 4 + jj
                            tensor.matmul(pw[4 + st][:], vb[:, tb, kv * 128:(kv + 1) * 128],
                                          attb[:, aslot, :],
                                          start=False, stop=True).then_inc(sem["smm"], 1)
                    inc("smm", 48)

                    @block.scalar
                    def _(scalar, sc_mark=sc_mark, av_mark=av_mark):
                        for j in range(12):
                            for st in range(2):
                                scalar.wait_ge(sem["smm"], sc_mark(st, j))
                                if j >= 3:
                                    scalar.wait_ge(sem["smm"], av_mark(st, j - 3))
                                scalar.activation(attb[:, 3 * st + j % 3, :],
                                                  pw[2 * st + j % 2][:], AF.Exp).then_inc(sem["sse"], 1)
                    inc("sse", 24)
                    prev_exp_end = cur("sse")

                    @block.vector
                    def _(vector, exp_mark=exp_mark):
                        for j in range(12):
                            for st in range(2):
                                vector.wait_ge(sem["sse"], exp_mark(st, j))
                                a = attb[:, 3 * st + j % 3, :]
                                if j < 4:
                                    vector.tensor_mul(a, a, msks[:, j, :]).then_inc(sem["sve"], 1)
                                elif j < 8:
                                    vector.tensor_scalar_mul(a, a, mscb[:, st * 4 + (j - 4):st * 4 + (j - 4) + 1]).then_inc(sem["sve"], 1)
                                else:
                                    vector.tensor_mul(a, a, msks[:, j - 4, :]).then_inc(sem["sve"], 1)
                    inc("sve", 24)

                    for st in range(2):
                        rrow = R_T0 if st == 0 else R_T1

                        @block.scalar
                        def _(scalar, st=st, rrow=rrow, m=av_mark(st, 11)):
                            scalar.wait_ge(sem["smm"], m)
                            scalar.activation(row(rrow), pr[st][0:1, :], AF.Ln)
                            scalar.activation(row(rrow), row(rrow), AF.Exp, scale=-1.0).then_inc(sem["sse"], 1)
                        wr1 = inc("sse", 1)

                        @block.tensor
                        def _(tensor, st=st, rrow=rrow, wr1=wr1, e=exp_mark(st, 11)):
                            tensor.wait_ge(sem["sse"], wr1)
                            tensor.wait_ge(sem["sse"], e)
                            tensor.matmul(pw[2 * st][:], ones_r[:], row(rrow), start=True, stop=True).then_inc(sem["smm"], 1)
                        wr2 = inc("smm", 1)

                        @block.vector
                        def _(vector, h=h, st=st, wr2=wr2):
                            vector.wait_ge(sem["smm"], wr2)
                            vector.tensor_copy(fold[:, st * 512:(st + 1) * 512], pw[2 * st][:])
                            vector.tensor_mul(qy[:, h, st * 512:(st + 1) * 512],
                                              pw[4 + st][:], fold[:, st * 512:(st + 1) * 512]).then_inc(sem["sve"], 1)
                        prev_y[st] = inc("sve", 1)
                y_done = cur("sve")

                if debug:
                    @block.gpsimd
                    def _(gpsimd, m=y_done):
                        gpsimd.wait_ge(sem["sve"], m)
                        gpsimd.dma_start(out=dbg["d_y"].rearrange("(t p) m -> p t m", p=128),
                                         in_=qy[:, :, :]).then_inc(sem["sbo"], 16)
                    inc("sbo", 16)

            pendo = []
            last_ev = 0
            for dt in range(NT):
                par = dt % 3
                w_rdy = stream_strip(wo_d, dt, wbf_a, par, "copy")
                pendo.append((dt, w_rdy, par))

                def compute_wo(dt, w_rdy, par):
                    nonlocal last_ev
                    marks = []

                    @block.tensor
                    def _(tensor, dt=dt, w_rdy=w_rdy, par=par, y_done=y_done, last_ev=last_ev):
                        tensor.wait_ge(sem[w_rdy[0]], w_rdy[1])
                        tensor.wait_ge(sem["sve"], y_done)
                        if last_ev:
                            tensor.wait_ge(sem["sve"], last_ev)
                        for s2 in range(NH):
                            for tc in range(2):
                                mm = tensor.matmul(pw[4 + tc][:], wbf_a[:, par, s2 * 128:(s2 + 1) * 128],
                                                   qy[:, s2, tc * 512:(tc + 1) * 512],
                                                   start=(s2 == 0), stop=(s2 == NH - 1))
                                if s2 == NH - 1:
                                    mm.then_inc(sem["smm"], 1)
                    marks.append(inc("smm", 1))
                    marks.append(inc("smm", 1))
                    buf_free[("wbf_a", par)] = marks[1]

                    @block.vector
                    def _(vector, dt=dt, marks=tuple(marks)):
                        vector.wait_ge(sem["smm"], marks[0])
                        vector.tensor_add(x_sb[:, dt, 0:512], x_sb[:, dt, 0:512], pw[4][:])
                        vector.wait_ge(sem["smm"], marks[1])
                        vector.tensor_add(x_sb[:, dt, 512:1024], x_sb[:, dt, 512:1024], pw[5][:]).then_inc(sem["sve"], 1)
                    last_ev = inc("sve", 1)

                if len(pendo) == 3:
                    compute_wo(*pendo.pop(0))
            while pendo:
                compute_wo(*pendo.pop(0))
        x2_done = cur("sve")

        if debug:
            @block.gpsimd
            def _(gpsimd, m=x2_done):
                gpsimd.wait_ge(sem["sve"], m)
                gpsimd.dma_start(out=dbg["d_x2"].rearrange("(t p) m -> p t m", p=128),
                                 in_=x_sb[:, :, :]).then_inc(sem["sbo"], 16)
            inc("sbo", 16)

        with nc.sbuf_tensor("xq2", [128, NT, TOK], BF16) as xq2, \
             nc.sbuf_tensor("p2", [128, SPQ, TOK], BF16) as p2:
            xq2_ready = emit_lnq(xq2, [("sve", x2_done)])
            emit_ffn_mats(xq2, p2, False, xq2_ready)

        st_fin = cur("sst")

        @block.gpsimd
        def _(gpsimd, st_fin=st_fin):
            gpsimd.wait_ge(sem["sst"], st_fin)

    return nc



def _tile_lhsT(W):
    outF, inF = W.shape
    S, KT = outF // 128, inF // 128
    t = W.reshape(S, 128, KT, 128)
    return np.ascontiguousarray(t.transpose(0, 3, 2, 1).reshape(S, 128, KT * 128))


def _rope_perm():
    p = np.empty(128, np.int64)
    p[:64] = np.arange(0, 128, 2)
    p[64:] = np.arange(1, 128, 2)
    return p


def _host_inputs(L0_attn_x, L0_attn_wq, L0_attn_wk, L0_attn_wv, L0_attn_wo,
                 L0_ffn_W_g, L0_ffn_W_u, L0_ffn_W_d, n_cores=8):
    wg_t = _tile_lhsT(np.asarray(L0_ffn_W_g))
    wu_t = _tile_lhsT(np.asarray(L0_ffn_W_u))
    Wd = np.asarray(L0_ffn_W_d)
    t = Wd.reshape(NT, 128, NQH, SPQ, 128)
    wd_t = np.ascontiguousarray(t.transpose(2, 0, 4, 3, 1).reshape(NSTRIP, 128, 2048))
    perm = _rope_perm()
    wq = np.asarray(L0_attn_wq).reshape(NH, 128, D)[:, perm, :].reshape(NH * 128, D)
    wk = np.asarray(L0_attn_wk).reshape(NKV, 128, D)[:, perm, :].reshape(NKV * 128, D)
    wq_t = _tile_lhsT(wq)
    wk_t = _tile_lhsT(wk)
    wv_t = np.ascontiguousarray(np.asarray(L0_attn_wv).T.reshape(NT, 128, 512))
    wo_t = _tile_lhsT(np.asarray(L0_attn_wo))
    swap = np.eye(128, dtype=np.float32)
    i_idx = np.arange(128)[:, None]
    c_idx = np.arange(512)[None, :]
    m_lo1 = np.stack([(c_idx <= j * 128 + i_idx - 1) for j in range(4)]).astype(ml_dtypes.bfloat16)
    m_lo0 = np.zeros_like(m_lo1)
    m_ca = np.stack([(c_idx >= j * 128 + i_idx) for j in range(4)]).astype(ml_dtypes.bfloat16)
    msk1 = np.ascontiguousarray(np.concatenate([m_lo1, m_ca], 0))
    msk0 = np.ascontiguousarray(np.concatenate([m_lo0, m_ca], 0))
    mscal1 = np.ones((1, 8), np.float32)
    mscal0 = np.array([[0, 0, 0, 0, 1, 1, 1, 1]], np.float32)
    inv = 1.0 / 10000.0 ** (np.arange(0, DH, 2, dtype=np.float64) / DH)
    x = np.asarray(L0_attn_x)
    in_maps = []
    for c in range(n_cores):
        b, hf = c // 2, c % 2
        t_abs = hf * TOK + np.arange(TOK)
        fr = np.outer(t_abs.astype(np.float64), inv)
        cos = np.cos(fr).astype(np.float32).T
        sin = np.sin(fr).astype(np.float32).T
        cs = np.empty((2, 128, TOK), np.float32)
        cs[0, :64] = cos
        cs[0, 64:] = cos
        cs[1, :64] = -sin
        cs[1, 64:] = sin
        xs = np.ascontiguousarray(x[b, hf * TOK:(hf + 1) * TOK, :].T)
        in_maps.append({
            "x": xs, "wg": wg_t, "wu": wu_t, "wd": wd_t,
            "wq": wq_t, "wk": wk_t, "wv": wv_t, "wo": wo_t,
            "msk": (msk1 if hf else msk0), "mscal": (mscal1 if hf else mscal0),
            "cs": cs, "swap": swap,
        })
    return in_maps


_NC_CACHE = {}


def kernel(L0_attn_x, L0_attn_wq, L0_attn_wk, L0_attn_wv, L0_attn_wo,
           L0_ffn_W_g, L0_ffn_W_u, L0_ffn_W_d):
    n_cores = 8
    in_maps = _host_inputs(L0_attn_x, L0_attn_wq, L0_attn_wk, L0_attn_wv,
                           L0_attn_wo, L0_ffn_W_g, L0_ffn_W_u, L0_ffn_W_d, n_cores)
    if n_cores not in _NC_CACHE:
        _NC_CACHE[n_cores] = build_program(n_cores=n_cores, debug=False)
    nc = _NC_CACHE[n_cores]
    res = run_bass_kernel_spmd(nc, in_maps, core_ids=list(range(n_cores)))
    out = np.empty((4, 2048, 2048), np.float32)
    for c in range(n_cores):
        b, hf = c // 2, c % 2
        out[b, hf * TOK:(hf + 1) * TOK, :] = np.asarray(res.results[c]["out"]).T
    return out
